# version 15
# baseline (speedup 1.0000x reference)
import numpy as np
import concourse.bass as bass
import concourse.mybir as mybir
from concourse.bass_utils import run_bass_kernel_spmd

F32 = mybir.dt.float32
BF16 = mybir.dt.bfloat16
U8 = mybir.dt.uint8
AF = mybir.ActivationFunctionType
ALU = mybir.AluOpType
AX = mybir.AxisListType

T = 2048
D = 1024
NCH = 8
NE = 8
HEADS = 16
NT = 4
TS = 512
EPS = 1e-5

ENGS = ("tensor", "scalar", "vector", "sync", "gpsimd")

STAGE = 99


def build(stage=None):
    if stage is None:
        stage = STAGE
    nc = bass.Bass("TRN2", target_bir_lowering=False)

    def din(name, shape, dtype=F32):
        return nc.dram_tensor(name, list(shape), dtype, kind="ExternalInput").ap()

    xT = din("xT", [D, T])
    WiT_d = din("WiT", [D, 128]); WsT_d = din("WsT", [D, 128]); WoT_d = din("WoT", [128, D])
    AT_d = din("AT", [128, 128]); BT_d = din("BT", [128, 128]); CT_d = din("CT", [128, 128])
    bi_d = din("bi", [128, 1]); bs_d = din("bs", [128, 1]); bo_d = din("bo", [128, 8])
    gwT_d = din("gwT", [D, NE]); gb_d = din("gateb", [1, NE])
    w1_d = din("w1", [NE, D, 4096])
    w2_d = din("w2", [NE, 2, 8, 2, 128, 8, 128])
    b1c_d = din("b1c", [128, NE, 32]); b2c_d = din("b2c", [128, NE, 8])
    WqT_d = din("WqT", [D, D]); WkT_d = din("WkT", [D, D])
    WvT_d = din("WvT", [D, D]); WuT_d = din("WuT", [D, D])
    qlw_d = din("qlw", [128, 1]); qlb_d = din("qlb", [128, 1])
    klw_d = din("klw", [128, 1]); klb_d = din("klb", [128, 1])
    selm_d = din("selm", [2, 128], BF16)
    mskb_d = din("mskb", [128, 2], BF16)
    WrT_d = din("WrT", [D, D]); WkfT_d = din("WkfT", [D, D]); WvfT_d = din("WvfT", [D, D])
    mur_d = din("mur", [128, 8]); muk_d = din("muk", [128, 8])
    maskA_d = din("maskA", [128, 128])
    maskB_d = din("maskB", [128, 128], BF16)
    ident_d = din("ident", [128, 128])
    out_d = nc.dram_tensor("out", [D, T], F32, kind="ExternalOutput").ap()

    base = (nc.sbuf_base + 63) & ~63
    assert base + 212832 <= nc.sbuf_top + 32, (base, nc.sbuf_top)

    def place(name, shape, dtype, off):
        hnd = nc.alloc_sbuf_tensor_at(name, list(shape), dtype, offset=base + off)
        return hnd[:]

    A_OFF, B_OFF, S_OFF, W_OFF = 0, 65536, 131072, 163840
    ROW0, ROW1, G8_OFF, M_OFF = 186368, 194560, 202752, 210944

    xA = place("xA", [128, NCH, T], F32, A_OFF)
    xB = place("xB", [128, NCH, T], F32, B_OFF)
    xS = place("xS", [128, NCH, T], BF16, S_OFF)
    row0 = place("row0", [1, T], F32, ROW0)
    row1 = place("row1", [1, T], F32, ROW1)
    brow_m = place("brow_m", [2, T], BF16, ROW0)
    brow_r = place("brow_r", [2, T], BF16, ROW0 + 4096)
    brow_g = place("brow_g", [1, T], BF16, ROW1)
    rowT = place("rowT", [2, T], F32, ROW1)
    g8 = place("g8", [8, T], F32, G8_OFF)
    glall = place("glall", [128, 16, NE], F32, W_OFF + 8192)

    mo = [M_OFF]
    def mplace(name, shape, dtype):
        h = place(name, shape, dtype, mo[0])
        per = int(np.prod(shape[1:])) * mybir.dt.size(dtype)
        mo[0] += ((per + 63) // 64) * 64
        return h
    ones = mplace("ones", [128, 1], F32)
    ones_b = mplace("ones_b", [128, 1], BF16)
    ones_r = mplace("ones_r", [1, 128], F32)
    ones_rb = mplace("ones_rb", [1, 128], BF16)
    gwS = mplace("gwS", [128, NCH, NE], F32)
    biS = mplace("biS", [128, 1], F32)
    bsS = mplace("bsS", [128, 1], F32)
    boS = mplace("boS", [128, 8], F32)
    gbS = mplace("gbS", [1, NE], F32)
    murS = mplace("murS", [128, 8], F32)
    mukS = mplace("mukS", [128, 8], F32)
    qlwS = mplace("qlwS", [128, 1], F32); qlbS = mplace("qlbS", [128, 1], F32)
    klwS = mplace("klwS", [128, 1], F32); klbS = mplace("klbS", [128, 1], F32)
    qlw8, qlb8 = qlwS, qlbS
    selm = place("selm", [2, 128], BF16, W_OFF + 21760)
    mskb = place("mskb", [128, 2], BF16, W_OFF + 22016)
    assert mo[0] <= 212832, mo[0]

    state_bf = place("state_bf", [128, T], BF16, W_OFF)
    ys_bf = place("ys_bf", [128, T], BF16, W_OFF + 4096)
    WiB = place("WiB", [128, NCH, 128], BF16, W_OFF + 8192)
    WsB = place("WsB", [128, NCH, 128], BF16, W_OFF + 10240)
    WoB = place("WoB", [128, D], BF16, B_OFF + 29184)
    AT = place("AT_s", [128, 128], BF16, B_OFF + 28672)
    BT = place("BT_s", [128, 128], BF16, W_OFF + 14592)
    CT = place("CT_s", [128, 128], BF16, B_OFF + 28928)
    CT2 = place("CT2_s", [128, 128], BF16, A_OFF)
    WoB2 = place("WoB2", [128, D], BF16, A_OFF + 8192)
    ident = place("ident_s", [128, 128], F32, W_OFF + 17408)
    zsq0 = place("zsq0", [128, T], BF16, W_OFF)
    zsq1 = place("zsq1", [128, T], BF16, W_OFF + 4096)
    gscr = place("gscr", [128, 1024], F32, W_OFF)
    hbuf = place("hbuf", [128, 16, TS], BF16, W_OFF)
    b1c = place("b1c", [128, NE, 32], F32, W_OFF + 16384)
    b2c = place("b2c", [128, NE, 8], F32, W_OFF + 17920)
    g_sb = place("g_sb", [128, TS], F32, W_OFF + 18432)
    ytmp = place("ytmp", [128, TS], F32, W_OFF + 20480)
    wqb = [place(f"wqb{i}", [128, NCH, 128], BF16, W_OFF + i * 2048) for i in range(2)]
    wkb = [place(f"wkb{i}", [128, NCH, 128], BF16, W_OFF + 4096 + i * 2048) for i in range(2)]
    wvb = [place(f"wvb{i}", [128, NCH, 128], BF16, W_OFF + 8192 + i * 2048) for i in range(2)]
    wub = [[place(f"wub{i}_{h}", [64, D], BF16, W_OFF + 12288 + (i * 2 + h) * 2048)
            for h in range(2)] for i in range(2)]
    maskA = place("maskA_s", [128, 128], F32, W_OFF + 20480)
    maskB = place("maskB_s", [128, 128], BF16, W_OFF + 20992)
    identB = place("identB_s", [128, 128], F32, W_OFF + 21248)
    wfb = [place(f"wfb{i}", [128, NCH, 128], BF16, W_OFF + i * 2048) for i in range(2)]
    tsub = place("tsub", [128, NCH, TS], BF16, W_OFF + 4096)
    tmpk = place("tmpk", [128, TS], BF16, W_OFF + 12288)
    tmpv = place("tmpv", [128, TS], F32, W_OFF + 13312)

    u_bf = place("u_bf", [128, T], BF16, B_OFF)
    sp_n = [place("sp_n0", [128, TS], F32, B_OFF + 8192),
            place("sp_n1", [128, TS], F32, B_OFF + 4096),
            place("sp_n2", [128, TS], F32, B_OFF + 6144),
            place("sp_n3", [128, TS], F32, B_OFF + 45056)]
    c_n = [place("c_n0", [128, TS], F32, B_OFF + 16384),
           place("c_n1", [128, TS], F32, B_OFF + 12288),
           place("c_n2", [128, TS], F32, B_OFF + 14336),
           place("c_n3", [128, TS], F32, B_OFF + 36864)]
    tanh_bf = place("tanh_bf", [128, T], BF16, B_OFF + 20480)
    sp_bf = place("sp_bf", [128, T], BF16, B_OFF + 24576)

    w1b = place("w1b", [128, NCH, 2048], BF16, A_OFF)
    w2c = place("w2c", [128, NCH, 2, 8, 128], BF16, A_OFF + 32768)
    W1F_OFF = [(B_OFF + 51200, B_OFF + 53248), (B_OFF + 55296, B_OFF + 59392),
               (B_OFF + 61440, B_OFF + 63488), (B_OFF + 18432, B_OFF + 26624),
               (S_OFF + 1024, S_OFF + 5120), (S_OFF + 9216, S_OFF + 13312),
               (S_OFF + 17408, S_OFF + 21504), (S_OFF + 25600, S_OFF + 29696)]
    w1f = [[place(f"w1f{dk}_{j}", [128, 1024], BF16, W1F_OFF[dk][j])
            for j in range(2)] for dk in range(NCH)]

    Qh = [place(f"Qh{h}", [65, T], BF16, A_OFF + h * 4096) for h in range(2)]
    Kh = [place(f"Kh{h}", [65, T], BF16, A_OFF + 8192 + h * 4096) for h in range(2)]
    Vg = place("Vg", [128, 16, 2, 65], BF16, A_OFF + 16384)
    PT = place("PT", [128, T], BF16, A_OFF + 20608)
    PT2 = place("PT2", [128, T], BF16, A_OFF + 57856)
    qraw = place("qraw", [128, T], F32, A_OFF + 24704)
    qpb = place("qpb", [128, T], BF16, A_OFF + 32896)
    q2pb = place("q2pb", [128, T], BF16, A_OFF + 36992)
    Qpair = place("Qpair", [128, T], BF16, A_OFF + 41088)
    aout = [place(f"aout{h}", [64, T], BF16, A_OFF + 45184 + h * 4096) for h in range(2)]
    mcols = place("mcols", [128, 16], F32, A_OFF + 53376)
    tmp16 = place("tmp16", [16, 128], BF16, A_OFF + 53504)
    mtmp = place("mtmp", [128, 128], F32, A_OFF + 53760)
    mcol2 = place("mcol2", [128, 1], F32, A_OFF + 54336)
    xrk = place("xrk", [128, NCH, T], BF16, A_OFF)
    rF = place("rF", [128, NCH, T], BF16, A_OFF + 32768)
    kact = place("kact", [128, NCH, T], BF16, S_OFF)
    ost = place("ost", [128, NCH, T], F32, A_OFF)

    psA = nc.psum_tensor("psA", [128, 2048], F32).__enter__()[:]
    psB = nc.psum_tensor("psB", [128, 2048], F32).__enter__()[:]

    def bank(i):
        p = psA if i < 4 else psB
        j = i % 4
        return p[:, j * TS:(j + 1) * TS]

    ops = {e: [] for e in ENGS}
    cnt = {}
    semnames = set()

    last_inc = {"vector": None, "scalar": None, "gpsimd": None}

    def E(eng, method, *args, w=(), s=None, n=1, chain=True, **kw):
        w = tuple(w)
        if eng in last_inc and method != "wait_" and chain:
            if last_inc[eng] is not None:
                w = w + (last_inc[eng],)
            if s is None:
                s = "q" + eng[0]
        for sn, _v in w:
            semnames.add(sn)
        if s is not None:
            semnames.add(s)
            cnt[s] = cnt.get(s, 0) + n
            if eng in last_inc:
                last_inc[eng] = (s, cnt[s])
        ops[eng].append((method, args, kw, w, s, n))
        return cnt.get(s)

    def V(s):
        return cnt.get(s, 0)

    nbar = [0]
    PH = []

    def mark(name):
        PH.append((name, {e: len(ops[e]) for e in ENGS}))
    build.PHASES = PH

    def barrier():
        nbar[0] += 1
        tgt = 5 * nbar[0]
        for eng in ENGS:
            E(eng, "drain", s="bar")
            E(eng, "wait_", w=(("bar", tgt),))
        cnt["bar"] = tgt
        last_inc["vector"] = None
        last_inc["scalar"] = None
        last_inc["gpsimd"] = None

    def pb(ap, n=128):
        return ap.partition_broadcast(n)

    for c in range(NCH):
        E("sync", "dma_start", out=xA[:, c, :], in_=xT[c * 128:(c + 1) * 128, :],
          s="ldx", n=16)
    for dst, srcd in ((ident, ident_d), (biS, bi_d), (bsS, bs_d), (boS, bo_d),
                      (gbS, gb_d), (murS, mur_d), (mukS, muk_d),
                      (qlwS, qlw_d), (qlbS, qlb_d), (klwS, klw_d), (klbS, klb_d),

                      (b1c, b1c_d), (b2c, b2c_d)):
        E("sync", "dma_start", out=dst, in_=srcd, s="ldm", n=16)
    E("sync", "dma_start", out=gwS, in_=gwT_d.rearrange("(c p) e -> p c e", p=128),
      s="ldm", n=16)
    for dst, srcd in ((AT, AT_d), (BT, BT_d), (CT, CT_d), (WoB, WoT_d)):
        E("gpsimd", "dma_start", out=dst, in_=srcd, s="ldc", n=16)
    E("gpsimd", "dma_start", out=WiB, in_=WiT_d.rearrange("(c p) m -> p c m", p=128),
      s="ldc", n=16)
    E("gpsimd", "dma_start", out=WsB, in_=WsT_d.rearrange("(c p) m -> p c m", p=128),
      s="ldc", n=16)
    E("vector", "memset", ones, 1.0)
    E("vector", "memset", ones_b, 1.0)
    E("vector", "memset", ones_r, 1.0)
    E("vector", "memset", ones_rb, 1.0)
    ld_all = (("ldx", V("ldx")), ("ldm", V("ldm")), ("ldc", V("ldc")))
    E("vector", "tensor_copy", w=ld_all, out=xS, in_=xA, s="shad")
    E("vector", "tensor_scalar_mul", qlwS, qlwS, 0.125, w=(("ldm", V("ldm")),))
    E("vector", "tensor_scalar_mul", qlbS, qlbS, 0.125)
    for eng in ("tensor", "scalar", "sync", "gpsimd"):
        E(eng, "wait_", w=ld_all)
    barrier()

    mark("P1_usp")
    for nt in range(NT):
        b = nt % 2
        lo = nt * TS
        for dk in range(NCH):
            E("tensor", "matmul", bank(b), WiB[:, dk, :], xS[:, dk, lo:lo + TS],
              start=(dk == 0), stop=(dk == NCH - 1),
              w=((("evu", max(0, V("evu") - 1)),) if dk == 0 else ()),
              s=("mmu" if dk == NCH - 1 else None))
        E("vector", "tensor_scalar_add", u_bf[:, lo:lo + TS], bank(b), biS,
          w=(("mmu", V("mmu")),), s="evu")
    for nt in range(NT):
        b = 2 + nt % 2
        lo = nt * TS
        for dk in range(NCH):
            E("tensor", "matmul", bank(b), WsB[:, dk, :], xS[:, dk, lo:lo + TS],
              start=(dk == 0), stop=(dk == NCH - 1),
              w=((("evs", max(0, V("evs") - 1)),) if dk == 0 else ()),
              s=("mms" if dk == NCH - 1 else None))
        E("vector", "tensor_scalar_add", sp_n[nt], bank(b), bsS,
          w=(("mms", V("mms")),), s="evs")
    for nt in range(NT):
        E("vector", "tensor_copy", out=sp_bf[:, nt * TS:(nt + 1) * TS],
          in_=sp_n[nt],
          w=((("evs", V("evs")),) if nt == 0 else ()),
          s=("spb" if nt == NT - 1 else None))
    for nt in range(NT):
        b = 4 + nt % 2
        lo = nt * TS
        E("tensor", "matmul", bank(b), BT, u_bf[:, lo:lo + TS], start=True, stop=False,
          w=(("evu", 4), ("spb", V("spb")), ("evc", max(0, V("evc") - 1))))
        if nt == 0:
            E("tensor", "matmul", bank(b)[:, 1:TS], AT, sp_bf[:, 0:TS - 1],
              start=False, stop=True, s="mmc")
        else:
            E("tensor", "matmul", bank(b), AT, sp_bf[:, lo - 1:lo + TS - 1],
              start=False, stop=True, s="mmc")
        E("vector", "tensor_copy", out=c_n[nt], in_=bank(b),
          w=(("mmc", V("mmc")),), s="evc")
    barrier()

    ssb = (psB[:, 0:1], psB[:, TS:TS + 1])

    def scan_step(t):
        if t == 0:
            E("scalar", "activation", tanh_bf[:, 0:1], c_n[0][:, 0:1], AF.Tanh,
              s="act", chain=False)
            return
        cbias = c_n[t // TS][:, t % TS:t % TS + 1]
        E("tensor", "matmul", ssb[t % 2], AT, tanh_bf[:, t - 1:t],
          start=True, stop=True, w=(("act", t),), s="spe")
        E("scalar", "activation", tanh_bf[:, t:t + 1], ssb[t % 2],
          AF.Tanh, bias=cbias, w=(("spe", t),), s="act", chain=False)

    CH_NTS = ((0,), (1, 2, 3))
    KA = 8
    mark("P2_scan")
    for t in range(0, TS):
        scan_step(t)
    scan_cursor = [TS]

    def scan_pump(nsteps):
        t0 = scan_cursor[0]
        t1 = min(T, t0 + nsteps)
        for t in range(t0, t1):
            scan_step(t)
        scan_cursor[0] = t1

    def l0out(ch):
        nts = CH_NTS[ch]
        CTu = CT if ch == 0 else CT2
        WoBu = WoB if ch == 0 else WoB2
        for i, nt in enumerate(nts):
            lo = nt * TS
            E("vector", "tensor_add", state_bf[:, lo:lo + TS],
              tanh_bf[:, lo:lo + TS], sp_n[nt],
              w=((("act", (nts[-1] + 1) * TS),) if i == 0 else ()), s="stb")
        stb_t = V("stb")
        for i, nt in enumerate(nts):
            b = i % 2
            lo = nt * TS
            E("tensor", "matmul", bank(b), CTu, state_bf[:, lo:lo + TS],
              start=True, stop=True,
              w=(("stb", stb_t), ("evy", max(0, V("evy") - 1)),
                 ("ldc2", V("ldc2"))), s="mmy")
            E("scalar", "activation", ys_bf[:, lo:lo + TS], bank(b), AF.Copy,
              w=(("mmy", V("mmy")),), s="evy")
        evy_t = V("evy")
        for o in range(NCH):
            for i, nt in enumerate(nts):
                b = 2 + (o * len(nts) + i) % 2
                lo = nt * TS
                E("tensor", "matmul", bank(b), WoBu[:, o * 128:(o + 1) * 128],
                  ys_bf[:, lo:lo + TS], start=True, stop=True,
                  w=(("evy", evy_t), ("evz", max(0, V("evz") - 1))), s="mmz")
                E("vector", "scalar_tensor_tensor", xB[:, o, lo:lo + TS], bank(b),
                  boS[:, o:o + 1], xA[:, o, lo:lo + TS], ALU.add, ALU.add,
                  w=(("mmz", V("mmz")), ("ldx2", V("ldx2"))), s="evz")
        E("vector", "drain", s="l0d")

    def emit_ln(src_t, dst, shadow, tag, nts=(0, 1, 2, 3), after=()):
        zs = (zsq0, zsq1)
        r0 = nts[0] * TS
        r1 = (nts[-1] + 1) * TS
        nchunk = r1 - r0
        for c in range(NCH):
            E("scalar", "activation", zs[c % 2][:, r0:r1], src_t[:, c, r0:r1],
              AF.Square,
              w=(("zrd", max(0, V("zrd") - 1)),) + (tuple(after) if c == 0 else ()),
              s="sq")
            for nt in nts:
                lo = nt * TS
                E("tensor", "matmul", psA[0:1, lo:lo + TS], ones,
                  src_t[:, c, lo:lo + TS], start=(c == 0), stop=(c == NCH - 1),
                  w=(tuple(after) if (c == 0 and nt == nts[0]) else ()))
                E("tensor", "matmul", psB[0:1, lo:lo + TS], ones_b,
                  zs[c % 2][:, lo:lo + TS], start=(c == 0), stop=(c == NCH - 1),
                  w=((("sq", V("sq")),) if nt == nts[0] else ()),
                  s=("zrd" if nt == nts[-1] else None))
        rowS = g8[0:1, r0:r1]
        E("vector", "tensor_scalar_mul", row1[:, r0:r1], psA[0:1, r0:r1], 1.0 / D,
          w=(("zrd", V("zrd")),))
        E("vector", "tensor_scalar_mul", row0[:, r0:r1], psB[0:1, r0:r1], 1.0 / D)
        E("vector", "tensor_tensor", rowS, row1[:, r0:r1], row1[:, r0:r1], ALU.mult)
        E("vector", "tensor_sub", row0[:, r0:r1], row0[:, r0:r1], rowS)
        E("vector", "tensor_scalar_add", row0[:, r0:r1], row0[:, r0:r1], EPS, s="vr")
        E("scalar", "activation", rowS, row0[:, r0:r1], AF.Ln,
          w=(("vr", V("vr")),))
        E("scalar", "activation", row0[:, r0:r1], rowS, AF.Exp, scale=-0.5,
          s="rs")
        E("vector", "tensor_tensor", row1[:, r0:r1], row1[:, r0:r1], row0[:, r0:r1],
          ALU.mult, w=(("rs", V("rs")),), s="lnw")
        lnw = V("lnw")
        for nt in nts:
            lo = nt * TS
            q = 4 + 2 * (nt % 2)
            E("tensor", "matmul", bank(q), ones_r, row0[:, lo:lo + TS],
              start=True, stop=True,
              w=(("lnw", lnw), ("apl", max(0, V("apl") - 1))))
            E("tensor", "matmul", bank(q + 1), ones_r, row1[:, lo:lo + TS],
              start=True, stop=True, s="bcl")
            for c in range(NCH):
                E("vector", "tensor_tensor", dst[:, c, lo:lo + TS],
                  src_t[:, c, lo:lo + TS], bank(q), ALU.mult,
                  w=((("bcl", V("bcl")),) if c == 0 else ()))
                E("vector", "tensor_tensor", dst[:, c, lo:lo + TS],
                  dst[:, c, lo:lo + TS], bank(q + 1), ALU.subtract)
                if shadow is not None:
                    E("vector", "tensor_copy", out=shadow[:, c, lo:lo + TS],
                      in_=dst[:, c, lo:lo + TS],
                      s=("apl" if c == NCH - 1 else None))
                elif c == NCH - 1:
                    E("vector", "drain", s="apl")
        E("vector", "drain", s="lnd")

    def gate(ch):
        nts = CH_NTS[ch]
        tt0 = nts[0] * 4
        ntt = len(nts) * 4
        for tt in range(tt0, tt0 + ntt):
            b = 4 + tt % 2
            pt = bank(b)[:, 0:NE]
            lo = tt * 128
            for dk in range(NCH):
                E("tensor", "matmul", pt, xA[:, dk, lo:lo + 128], gwS[:, dk, :],
                  start=(dk == 0), stop=False,
                  w=((("ggl", max(0, V("ggl") - 1)), ("apl", V("apl")))
                     if dk == 0 else ()))
            E("tensor", "matmul", pt, ones_r, gbS, start=False, stop=True, s="gmm")
            E("vector", "tensor_copy", out=glall[:, tt, :], in_=pt,
              w=(("gmm", V("gmm")),), s="ggl")
        gp3 = gscr[:, 0:8 * ntt].rearrange("p (t e) -> p t e", t=ntt)
        gm3 = gscr[:, 512:512 + 8 * ntt].rearrange("p (t e) -> p t e", t=ntt)
        gx = gscr[:, 256:256 + ntt].rearrange("p (t o) -> p t o", o=1)
        gv = gscr[:, 768:768 + ntt].rearrange("p (t o) -> p t o", o=1)
        gsl = glall[:, tt0:tt0 + ntt, :]

        def bc8(ap):
            return ap.to_broadcast([128, ntt, NE])

        E("vector", "reduce_max", gx, gsl, AX.X, s="gmx")
        E("vector", "tensor_tensor", gp3, gsl, bc8(gx), ALU.subtract)
        E("scalar", "activation", gp3, gp3, AF.Exp, w=(("qv", V("qv")),), s="gex")
        E("vector", "reduce_sum", gv, gp3, AX.X, w=(("gex", V("gex")),))
        E("vector", "reciprocal", gx, gv)
        E("vector", "tensor_tensor", gp3, gp3, bc8(gx), ALU.mult)
        E("vector", "reduce_max", gv, gp3, AX.X)
        E("vector", "tensor_tensor", gm3, gp3, bc8(gv), ALU.is_equal)
        E("vector", "scalar_tensor_tensor", gm3, gm3, -1e30, gp3, ALU.mult, ALU.add)
        E("vector", "reduce_max", gv, gm3, AX.X)
        E("vector", "tensor_tensor", gm3, gp3, bc8(gv), ALU.is_ge)
        E("vector", "tensor_mul", gsl, gp3, gm3, s="gtile")
        for tt in range(tt0, tt0 + ntt):
            lo = tt * 128
            E("tensor", "transpose", psB[0:NE, 1024:1152], glall[:, tt, :], ident,
              w=(("gtile", V("gtile")), ("gev", V("gev"))), s="gtr")
            E("vector", "tensor_copy", out=g8[:, lo:lo + 128],
              in_=psB[0:NE, 1024:1152],
              w=(("gtr", V("gtr")),), s="gev")
        E("vector", "tensor_copy",
          out=xB[:, :, nts[0] * TS:(nts[-1] + 1) * TS],
          in_=xA[:, :, nts[0] * TS:(nts[-1] + 1) * TS], s="zin")

    def w1_load(e, half, w=()):
        for dk in range(NCH):
            E("gpsimd", "dma_start", out=w1b[:, dk, :],
              in_=w1_d[e, dk * 128:(dk + 1) * 128, half * 2048:(half + 1) * 2048],
              w=(w if dk == 0 else ()), s="w1l", n=16)

    def w1f_load(e, half, w=()):
        for dk in range(NCH):
            for j in range(2):
                c0 = half * 2048 + j * 1024
                E("gpsimd", "dma_start", out=w1f[dk][j],
                  in_=w1_d[e, dk * 128:(dk + 1) * 128, c0:c0 + 1024],
                  w=(w if (dk == 0 and j == 0) else ()), s="w1l", n=16)

    def w2_load(e, half, w=(), wo=None):
        done = []
        for o in range(NCH):
            wait = w if o == 0 else ()
            if wo is not None:
                wait = wait + (("mm2", wo[o]),)
            E("gpsimd", "dma_start", out=w2c[:, o],
              in_=w2_d[e, half, o].rearrange("j p h o -> p j h o"),
              w=wait, s="w2a", n=16)
            done.append(V("w2a"))
        return done

    HALVES = [(e, h) for e in range(NE) for h in range(2)]

    def moe(nts, halves, pp, hook=None):
        w1done = {}
        zw = (("zin", V("zin")),)
        if V("mm1") > 0:
            zw = zw + (("mm1", V("mm1")),)
        w1_load(halves[0][0], halves[0][1], w=zw)
        w1done[0] = V("w1l")
        mm2_slice = None
        for hi, (e, half) in enumerate(halves):
            if hi == 0:
                w2w = zw + ((("mm2", V("mm2")),) if V("mm2") > 0 else ())
                w2done_h = w2_load(e, half, w=w2w)
            else:
                w2done_h = w2_load(e, half, wo=mm2_slice)
            mm2_slice = [0] * NCH
            if pp and hi + 1 < len(halves):
                e2, h2 = halves[hi + 1]
                wn = zw if hi == 0 else (("mm1", V("mm1")),)
                if (hi + 1) % 2 == 0:
                    w1_load(e2, h2, w=wn)
                else:
                    w1f_load(e2, h2, w=wn)
                w1done[hi + 1] = V("w1l")
            w1_tgt = w1done[hi] if pp else V("w1l")
            if half == 0:
                cl = nts[0] * TS
                cw = len(nts) * TS
                E("sync", "dma_start", out=row0[:, cl:cl + cw],
                  in_=g8[e:e + 1, cl:cl + cw],
                  w=(("gbc", V("gbc")),), s="grw", n=16)
                E("vector", "tensor_copy", out=brow_g[:, cl:cl + cw],
                  in_=row0[:, cl:cl + cw],
                  w=(("grw", V("grw")), ("gbc", V("gbc"))), s="gcv")
            gcv_t = V("gcv")
            for nt in nts:
                lo = nt * TS
                gq = 6 + nt % 2
                E("tensor", "matmul", bank(gq), ones_rb, brow_g[:, lo:lo + TS],
                  start=True, stop=True,
                  w=(("gsc", max(0, V("gsc") - 1)), ("gcv", gcv_t)), s="gbc")
                E("scalar", "activation", g_sb, bank(gq), AF.Copy,
                  w=(("gbc", V("gbc")), ("ye", V("ye"))), s="gsc")
                gsb_tgt = V("gsc")
                frag = pp and hi % 2 == 1
                for hk in range(16):
                    b = hk % 2
                    for dk in range(NCH):
                        hsem = "hes" if hk % 2 == 0 else "hev"
                        wt1 = (w1f[dk][hk // 8][:, (hk % 8) * 128:
                                                (hk % 8 + 1) * 128] if frag
                               else w1b[:, dk, hk * 128:(hk + 1) * 128])
                        E("tensor", "matmul", bank(b), wt1,
                          xS[:, dk, lo:lo + TS], start=(dk == 0),
                          stop=(dk == NCH - 1),
                          w=((("w1l", w1_tgt), (hsem, V(hsem)))
                             if dk == 0 else ()),
                          s=("mm1" if dk == NCH - 1 else None))
                        if hook is not None:
                            hook()
                    b1col = b1c[:, e, half * 16 + hk:half * 16 + hk + 1]
                    if hk % 2 == 0:
                        E("scalar", "activation", hbuf[:, hk, :], bank(b), AF.Relu,
                          bias=b1col, w=(("mm1", V("mm1")), ("mm2", V("mm2"))),
                          s="hes")
                    else:
                        E("vector", "tensor_scalar", hbuf[:, hk, :], bank(b), b1col,
                          0.0, ALU.add, ALU.max,
                          w=(("mm1", V("mm1")), ("mm2", V("mm2"))), s="hev")
                    if hook is not None:
                        hook()
                he_tgt = (("hes", V("hes")), ("hev", V("hev")))
                for o in range(NCH):
                    b = 2 + o % 2
                    w2s = ("w2a", w2done_h[o])
                    for hk in range(16):
                        E("tensor", "matmul", bank(b),
                          w2c[:, o, hk // 8, hk % 8, :],
                          hbuf[:, hk, :], start=(hk == 0), stop=(hk == 15),
                          w=(((w2s, ("ye", max(0, V("ye") - 1)))
                              + he_tgt) if hk == 0 else ()),
                          s=("mm2" if hk == 15 else None))
                    mm2_slice[o] = V("mm2")
                    b2col = b2c[:, e, o:o + 1] if half == 0 else 0.0
                    E("vector", "scalar_tensor_tensor", ytmp, bank(b), b2col,
                      g_sb, ALU.add, ALU.mult,
                      w=(("mm2", V("mm2")), ("gsc", gsb_tgt)))
                    E("vector", "tensor_add", xB[:, o, lo:lo + TS],
                      xB[:, o, lo:lo + TS], ytmp, s="ye")
                    if hook is not None:
                        hook()
            if not pp and hi + 1 < len(halves):
                e2, h2 = halves[hi + 1]
                w1_load(e2, h2, w=(("mm1", V("mm1")),))

    if stage < 4:
        scan_pump(T - TS)
    for ch in (0, 1):
        if ch == 1 and stage >= 4:
            for c in range(NCH):
                E("sync", "dma_start", out=xA[:, c, TS:T],
                  in_=xT[c * 128:(c + 1) * 128, TS:T], s="ldx2", n=16)
            E("gpsimd", "dma_start", out=CT2, in_=CT_d, s="ldc2", n=16)
            E("gpsimd", "dma_start", out=WoB2, in_=WoT_d, s="ldc2", n=16)
        mark(f"l0out_c{ch}")
        l0out(ch)
        mark(f"ln1_c{ch}")
        emit_ln(xB, xA, xS, "l1", nts=CH_NTS[ch],
                after=(("l0d", V("l0d")),))
        mark(f"gate_c{ch}")
        gate(ch)
        if stage >= 4:
            barrier()
            if ch == 0:
                mark("moe_c0")
                nhooks = KA * (16 * 9 + 8)
                per = [0]

                def hook(per=per):
                    per[0] += 1
                    tgt = TS + ((T - TS) * per[0]) // nhooks
                    scan_pump(tgt - scan_cursor[0])
                moe(CH_NTS[0], HALVES[:KA], pp=True, hook=hook)
                scan_pump(T - scan_cursor[0])
            else:
                mark("moe_mix")
                moe((0, 1, 2, 3), HALVES[KA:], pp=False)
                mark("moe_c1")
                moe(CH_NTS[1], HALVES[:KA], pp=False)
            barrier()
    if stage == 1:
        return finish(nc, E, V, ops, semnames, xB, out_d, NCH)
    if stage in (2, 3):
        return finish(nc, E, V, ops, semnames, xA, out_d, NCH)
    mark("LN2")
    emit_ln(xB, xA, xS, "l2")
    barrier()
    if stage == 4:
        return finish(nc, E, V, ops, semnames, xA, out_d, NCH)

    mark("P8_attn")
    E("vector", "tensor_copy", out=xB, in_=xA, s="zpin")
    E("sync", "dma_start", out=maskA, in_=maskA_d, s="lda", n=16)
    E("sync", "dma_start", out=maskB, in_=maskB_d, s="lda", n=16)
    E("sync", "dma_start", out=identB, in_=ident_d, s="lda", n=16)
    E("sync", "dma_start", out=selm, in_=selm_d, s="lda", n=16)
    E("sync", "dma_start", out=mskb, in_=mskb_d, s="lda", n=16)
    lda_t = V("lda")
    barrier()

    def attw_load(g):
        p = g % 2
        wt = (("wmm", max(0, (g - 1) * 32)),)
        first = True
        for mat_d, dstb in ((WqT_d, wqb[p]), (WkT_d, wkb[p]), (WvT_d, wvb[p])):
            for dh in range(2):
                E("gpsimd", "dma_start",
                  out=dstb[:, dh * 4:(dh + 1) * 4, :],
                  in_=mat_d[dh * 512:(dh + 1) * 512, g * 128:(g + 1) * 128]
                  .rearrange("(dk p) m -> p dk m", p=128),
                  w=(wt if first else ()), s=f"awl{p}", n=16)
                first = False
        for h in range(2):
            E("gpsimd", "dma_start", out=wub[p][h],
              in_=WuT_d[g * 128 + h * 64:g * 128 + (h + 1) * 64, :],
              s=f"awl{p}", n=16)

    attw_load(0)
    attw_load(1)
    AWL_PER_G = 8 * 16

    for g in range(8):
        p = g % 2
        awl_w = (f"awl{g % 2}", AWL_PER_G * (g // 2 + 1))
        mark(f"attn_heads_g{g}")
        mark(f"g{g}_proj")
        qev_base = V("qev")
        for which in ("q", "k"):
            wbuf = wqb[p] if which == "q" else wkb[p]
            for nt in range(NT):
                lo = nt * TS
                prev_evac = qev_base + nt + 1 - (4 if which == "q" else 0)
                for dk in range(NCH):
                    E("tensor", "matmul", bank(nt), wbuf[:, dk, :],
                      xS[:, dk, lo:lo + TS],
                      start=(dk == 0), stop=(dk == NCH - 1),
                      w=((("qev", max(0, prev_evac)), ("vev", V("vev")),
                          ("aev", V("aev")), awl_w,
                          ("wad", max(0, V("wad") - 3 + nt)))
                         if dk == 0 else ()),
                      s=("qmm" if dk == NCH - 1 else None))
        E("vector", "memset", Vg[:, :, :, 64:65], 1.0,
          w=(("amm", V("amm")),), s="vms")
        for tt in range(16):
            b = 4 + tt % 2
            for dk in range(NCH):
                E("tensor", "matmul", bank(b)[:, 0:128],
                  xS[:, dk, tt * 128:(tt + 1) * 128], wvb[p][:, dk, :],
                  start=(dk == 0), stop=(dk == NCH - 1),
                  w=((awl_w, ("vev", max(0, V("vev") - 1)))
                     if dk == 0 else ()),
                  s=("vmm" if dk == NCH - 1 else None))
            E("vector", "tensor_copy",
              out=Vg[:, tt, :, 0:64],
              in_=bank(b)[:, 0:128].rearrange("p (h s) -> p h s", h=2),
              w=(("vmm", V("vmm")), ("vms", V("vms"))), s="vev")
        for which in ("q", "k"):
            dst0 = Qh if which == "q" else Kh
            wcol = qlw8 if which == "q" else klwS
            bcol = qlb8 if which == "q" else klbS
            qmm_t = V("qmm") - (NT if which == "q" else 0)
            for nt in range(NT):
                lo = nt * TS
                E("scalar", "activation", qraw[:, lo:lo + TS], bank(nt),
                  AF.Copy, w=(("qmm", qmm_t - NT + nt + 1), ("qcb", V("qcb")),
                              ("hvr", V("hvr"))), s="qev")
            E("scalar", "activation", q2pb, qraw, AF.Square,
              w=(("qev", V("qev")), ("smm", V("smm"))), s="sq2")
            E("vector", "tensor_copy", out=qpb, in_=qraw,
              w=(("qev", V("qev")), ("smm", V("smm"))), s="qcb")
            for nt in range(NT):
                lo = nt * TS
                E("tensor", "matmul", psB[0:2, lo:lo + TS], mskb,
                  qpb[:, lo:lo + TS], start=True, stop=True,
                  w=((("qcb", V("qcb")),
                      ("hvm", V("hvm")), ("lda", lda_t),
                      ("vev", V("vev")), ("aev", V("aev"))) if nt == 0 else ()),
                  s=("smm" if nt == NT - 1 else None))
            E("vector", "tensor_scalar_mul", rowT, psB[0:2, :], 1.0 / 64,
              w=(("smm", V("smm")), ("hrs", V("hrs")), ("rlw", V("rlw"))),
              s="hvm")
            for nt in range(NT):
                lo = nt * TS
                E("tensor", "matmul", psB[0:2, lo:lo + TS], mskb,
                  q2pb[:, lo:lo + TS], start=True, stop=True,
                  w=((("sq2", V("sq2")), ("hvm", V("hvm"))) if nt == 0 else ()),
                  s=("smm" if nt == NT - 1 else None))
            rowS = qraw[0:2, :]
            E("vector", "tensor_scalar_mul", rowS, psB[0:2, :], 1.0 / 64,
              w=(("smm", V("smm")),), s="hvm")
            E("vector", "tensor_copy", out=brow_m, in_=rowT,
              w=(("hbc", V("hbc")), ("rlb", V("rlb"))), s="hmw")
            E("vector", "tensor_mul", rowT, rowT, rowT)
            E("vector", "tensor_sub", rowS, rowS, rowT)
            E("vector", "tensor_scalar_add", rowS, rowS, EPS, s="hvr")
            E("scalar", "activation", rowT, rowS, AF.Ln,
              w=(("hvr", V("hvr")),))
            E("scalar", "activation", brow_r, rowT, AF.Exp,
              scale=-0.5, s="hrs")
            hlw_t = V("hrs")
            for nt in range(NT):
                lo = nt * TS
                q2 = 4 + 2 * (nt % 2)
                E("tensor", "matmul", bank(q2), selm, brow_r[:, lo:lo + TS],
                  start=True, stop=True,
                  w=(("hrs", hlw_t), ("hmw", V("hmw")), ("lda", lda_t),
                     ("hap", max(0, V("hap") - 1))))
                E("tensor", "matmul", bank(q2 + 1), selm, brow_m[:, lo:lo + TS],
                  start=True, stop=True, s="hbc")
                E("vector", "tensor_tensor", Qpair[:, lo:lo + TS],
                  qpb[:, lo:lo + TS], bank(q2 + 1), ALU.subtract,
                  w=(("hbc", V("hbc")), ("qdl", V("qdl"))))
                E("vector", "tensor_tensor", Qpair[:, lo:lo + TS],
                  Qpair[:, lo:lo + TS], bank(q2), ALU.mult, s="hap")
            E("vector", "tensor_scalar", Qpair, Qpair, wcol, bcol,
              ALU.mult, ALU.add, w=(("mB", V("mB")), ("hap", V("hap"))),
              s="qap")
            for hl2 in range(2):
                E("sync", "dma_start", out=dst0[hl2][0:64, :],
                  in_=Qpair[hl2 * 64:(hl2 + 1) * 64, :],
                  w=((("qap", V("qap")), ("mB", V("mB"))) if hl2 == 0 else ()),
                  s="qdl", n=16)
            if which == "k":
                E("vector", "memset", Kh[0][64:65, :], 1.0, s="kap")
                E("vector", "memset", Kh[1][64:65, :], 1.0, s="kap")
        qdl_t = V("qdl")
        for hl in range(2):
            h = 2 * g + hl
            mark(f"h{g}_{hl}_passA")
            pass
            for qt in range(16):
                kext = (qt + 1) * 128
                ps = psA if qt % 2 == 0 else psB
                nsub = (kext + TS - 1) // TS
                for j in range(nsub):
                    n0 = j * TS
                    n1 = min(kext, n0 + TS)
                    E("tensor", "matmul", ps[:, n0:n1],
                      Qh[hl][0:64, qt * 128:(qt + 1) * 128], Kh[hl][0:64, n0:n1],
                      start=True, stop=True,
                      w=((("qdl", qdl_t), ("lda", lda_t),
                          ("mxA", max(0, V("mxA") - 1)),
                          ("aev", V("aev"))) if j == 0 else ()),
                      s=("mA" if j == nsub - 1 else None))
                E("vector", "tensor_add", mtmp, ps[:, qt * 128:kext],
                  maskA, w=(("mA", V("mA")),))
                if qt == 0:
                    E("vector", "reduce_max", mcols[:, 0:1], mtmp, AX.X, s="mxA")
                else:
                    E("vector", "reduce_max", mcols[:, qt:qt + 1],
                      ps[:, 0:qt * 128], AX.X)
                    E("vector", "reduce_max", mcol2, mtmp, AX.X)
                    E("vector", "tensor_max", mcols[:, qt:qt + 1],
                      mcols[:, qt:qt + 1], mcol2, s="mxA")
            mark(f"h{g}_{hl}_mrow")
            E("tensor", "transpose", psA[0:16, 0:128], mcols, identB,
              w=(("mxA", V("mxA")),), s="mtr")
            E("vector", "tensor_scalar_mul", tmp16, psA[0:16, 0:128], -1.0,
              w=(("mtr", V("mtr")),), s="mng")
            E("sync", "dma_start", out=Qh[hl][64:65, :], in_=tmp16,
              w=(("mng", V("mng")),), s="mdl", n=16)
            mark(f"h{g}_{hl}_passB")
            mdl_t = V("mdl")
            expcnt = {0: V("xpb"), 1: V("xpb")}
            jexp_all = {}
            ptm_all = {}

            def emit_attn(kt):
                q0 = kt * 128
                PTb = PT if kt % 2 == 0 else PT2
                for j in range(kt // 4, 4):
                    s0 = max(q0, j * TS)
                    s1 = (j + 1) * TS
                    ww = [("xpb", jexp_all[(kt, j)])]
                    if j == kt // 4:
                        ww.append(("ptm", ptm_all[kt]))
                    if kt == 0:
                        ww.append(("vev", V("vev")))
                        ww.append(("aev", V("aev")))
                    E("tensor", "matmul", psB[0:65, s0:s1], Vg[:, kt, hl, :],
                      PTb[:, s0:s1], start=(kt == 0), stop=(kt == 4 * j + 3),
                      w=tuple(ww), s=("amm" if j == 3 else None))

            for kt in range(16):
                q0 = kt * 128
                first_j = kt // 4
                jlist = list(range(first_j, 4))
                PTb = PT if kt % 2 == 0 else PT2
                segs = []
                if kt < 8:
                    ja = [j for j in jlist if j < 2]
                    jb = [j for j in jlist if j >= 2]
                    if ja:
                        segs.append((0, max(q0, ja[0] * TS), 1024, ja))
                    segs.append((1, max(q0, jb[0] * TS), 2048, jb))
                else:
                    grp = kt % 2
                    for j in jlist:
                        segs.append((grp, max(q0, j * TS), (j + 1) * TS, [j]))
                for grp, c0, c1, js in segs:
                    for idx, j in enumerate(js):
                        s0 = max(q0, j * TS)
                        s1 = (j + 1) * TS
                        p0 = s0 if kt < 8 else grp * 1024 + (s0 - c0)
                        E("tensor", "matmul", psA[:, p0:p0 + (s1 - s0)],
                          Kh[hl][0:65, q0:q0 + 128], Qh[hl][0:65, s0:s1],
                          start=True, stop=True,
                          w=((("mdl", mdl_t), ("xpb", expcnt[grp]),
                              ("qdl", qdl_t), ("kap", V("kap")))
                             if idx == 0 else ()),
                          s="mB")
                    p0 = c0 if kt < 8 else grp * 1024
                    E("scalar", "activation", PTb[:, c0:c1],
                      psA[:, p0:p0 + (c1 - c0)], AF.Exp,
                      w=(("mB", V("mB")),), s="xpb")
                    expcnt[grp] = V("xpb")
                    for j in js:
                        jexp_all[(kt, j)] = V("xpb")
                    if c0 <= q0 < c1:
                        E("vector", "tensor_mul", PTb[:, q0:q0 + 128],
                          PTb[:, q0:q0 + 128], maskB,
                          w=(("xpb", V("xpb")),), s="ptm")
                        ptm_all[kt] = V("ptm")
                if kt > 0:
                    emit_attn(kt - 1)
            emit_attn(15)
            mark(f"h{g}_{hl}_evac")
            amm_t = V("amm")
            E("scalar", "activation", rowT[0:1, :], psB[64:65, :], AF.Ln,
              w=(("amm", amm_t),))
            E("scalar", "activation", brow_m[0:1, :], rowT[0:1, :], AF.Exp,
              scale=-1.0, s="rlw")
            for j in range(4):
                c0 = j * TS
                E("vector", "tensor_copy", out=aout[hl][:, c0:c0 + TS],
                  in_=psB[0:64, c0:c0 + TS],
                  w=((("amm", amm_t),) if j == 0 else ()),
                  s=("acp" if j == 3 else None))
            for j in range(4):
                c0 = j * TS
                E("tensor", "matmul", psA[:, c0:c0 + TS], ones_rb,
                  brow_m[0:1, c0:c0 + TS], start=True, stop=True,
                  w=((("rlw", V("rlw")),) if j == 0 else ()),
                  s=("rlb" if j == 3 else None))
            E("vector", "tensor_mul", aout[hl], aout[hl], psA[0:64, :],
              w=(("rlb", V("rlb")), ("acp", V("acp"))), s="aev")
        mark(f"g{g}_wu")
        aev_t = V("aev")
        for o in range(NCH):
            for nt in range(NT):
                b = (o * NT + nt) % 4
                lo = nt * TS
                E("tensor", "matmul", bank(b), wub[p][0][:, o * 128:(o + 1) * 128],
                  aout[0][:, lo:lo + TS], start=True, stop=False,
                  w=(("aev", aev_t), ("wad", max(0, V("wad") - 3))))
                E("tensor", "matmul", bank(b), wub[p][1][:, o * 128:(o + 1) * 128],
                  aout[1][:, lo:lo + TS], start=False, stop=True, s="wmm")
                E("vector", "tensor_add", xB[:, o, lo:lo + TS], xB[:, o, lo:lo + TS],
                  bank(b), w=(("wmm", V("wmm")),), s="wad")
        if g + 2 < 8:
            attw_load(g + 2)
    barrier()
    emit_ln(xB, xA, xS, "l3")
    barrier()
    if stage == 5:
        return finish(nc, E, V, ops, semnames, xA, out_d, NCH)

    mark("P10_ffn")
    E("vector", "tensor_copy", out=xB, in_=xA, s="z2in")

    def mix(dst_name, muS, sem, w0=()):
        for nt in range(NT):
            lo = nt * TS
            for c in range(NCH):
                if nt == 0:
                    E("vector", "tensor_sub", tsub[:, c, 1:TS],
                      xS[:, c, 1:TS], xS[:, c, 0:TS - 1],
                      w=(w0 if c == 0 else ()))
                    E("vector", "scalar_tensor_tensor", xrk[:, c, lo + 1:lo + TS],
                      tsub[:, c, 1:TS], muS[:, c:c + 1], xS[:, c, 0:TS - 1],
                      ALU.mult, ALU.add)
                    E("vector", "tensor_single_scalar", xrk[:, c, 0:1],
                      xS[:, c, 0:1], muS[:, c:c + 1], ALU.mult)
                else:
                    E("vector", "tensor_sub", tsub[:, c, :],
                      xS[:, c, lo:lo + TS], xS[:, c, lo - 1:lo + TS - 1])
                    E("vector", "scalar_tensor_tensor", xrk[:, c, lo:lo + TS],
                      tsub[:, c, :], muS[:, c:c + 1], xS[:, c, lo - 1:lo + TS - 1],
                      ALU.mult, ALU.add,
                      s=(sem if (nt == NT - 1 and c == NCH - 1) else None))

    def ffw_load(mat_d, o, w=()):
        E("gpsimd", "dma_start", out=wfb[o % 2],
          in_=mat_d[:, o * 128:(o + 1) * 128].rearrange("(dk p) m -> p dk m", p=128),
          w=w, s=f"fwl{o % 2}", n=16)

    mix("r", murS, "xrd")
    ffw_load(WrT_d, 0)
    ffw_load(WrT_d, 1)
    for o in range(NCH):
        fw_w = (f"fwl{o % 2}", 16 * (0 * 4 + o // 2 + 1))
        for nt in range(NT):
            b = nt % 2
            lo = nt * TS
            for dk in range(NCH):
                E("tensor", "matmul", bank(b), wfb[o % 2][:, dk, :],
                  xrk[:, dk, lo:lo + TS], start=(dk == 0), stop=(dk == NCH - 1),
                  w=((fw_w, ("xrd", 1),
                      ("rev", max(0, V("rev") - 1))) if dk == 0 else ()),
                  s=("rmm" if dk == NCH - 1 else None))
            E("scalar", "activation", rF[:, o, lo:lo + TS], bank(b), AF.Sigmoid,
              w=(("rmm", V("rmm")),), s="rev")
        if o + 2 < NCH:
            ffw_load(WrT_d, o + 2, w=(("rmm", V("rmm")),))
        elif o == NCH - 2:
            ffw_load(WkfT_d, 0, w=(("rmm", V("rmm")),))
        elif o == NCH - 1:
            ffw_load(WkfT_d, 1, w=(("rmm", V("rmm")),))
    mix("k", mukS, "xkd", w0=(("rmm", 32),))
    for o in range(NCH):
        fw_w = (f"fwl{o % 2}", 16 * (1 * 4 + o // 2 + 1))
        for nt in range(NT):
            b = nt % 2
            lo = nt * TS
            for dk in range(NCH):
                E("tensor", "matmul", bank(b), wfb[o % 2][:, dk, :],
                  xrk[:, dk, lo:lo + TS], start=(dk == 0), stop=(dk == NCH - 1),
                  w=((fw_w, ("xkd", 1), ("rev", V("rev")),
                      ("kev", max(0, V("kev") - 1))) if dk == 0 else ()),
                  s=("kmm" if dk == NCH - 1 else None))
            E("scalar", "activation", tmpk, bank(b), AF.Relu,
              w=(("kmm", V("kmm")), ("kev", V("kev"))), s="krl")
            E("vector", "tensor_tensor", kact[:, o, lo:lo + TS], tmpk, bank(b),
              ALU.mult, w=(("krl", V("krl")),), s="kev")
        if o + 2 < NCH:
            ffw_load(WkfT_d, o + 2, w=(("kmm", V("kmm")),))
        elif o == NCH - 2:
            ffw_load(WvfT_d, 0, w=(("kmm", V("kmm")),))
        elif o == NCH - 1:
            ffw_load(WvfT_d, 1, w=(("kmm", V("kmm")),))
    kev_all_t = V("kev")
    for o in range(NCH):
        fw_w = (f"fwl{o % 2}", 16 * (2 * 4 + o // 2 + 1))
        for nt in range(NT):
            b = nt % 2
            lo = nt * TS
            for dk in range(NCH):
                E("tensor", "matmul", bank(b), wfb[o % 2][:, dk, :],
                  kact[:, dk, lo:lo + TS], start=(dk == 0), stop=(dk == NCH - 1),
                  w=((fw_w, ("kev", kev_all_t),
                      ("vad", max(0, V("vad") - 1))) if dk == 0 else ()),
                  s=("fvm" if dk == NCH - 1 else None))
            E("vector", "tensor_tensor", tmpv, bank(b), rF[:, o, lo:lo + TS],
              ALU.mult, w=(("fvm", V("fvm")),))
            E("vector", "tensor_add", xB[:, o, lo:lo + TS], xB[:, o, lo:lo + TS],
              tmpv, s="vad")
        if o + 2 < NCH:
            ffw_load(WvfT_d, o + 2, w=(("fvm", V("fvm")),))
    barrier()
    mark("LN4")
    emit_ln(xB, ost, None, "l4")
    for c in range(NCH):
        E("sync", "dma_start", out=out_d[c * 128:(c + 1) * 128, :], in_=ost[:, c, :],
          w=(("lnd", V("lnd")),), s="st", n=16)
    E("sync", "wait_", w=(("st", V("st")),))
    return realize(nc, ops, semnames)


def finish(nc, E, V, ops, semnames, src, out_d, nch):
    for c in range(nch):
        E("sync", "dma_start", out=out_d[c * 128:(c + 1) * 128, :], in_=src[:, c, :],
          s="st", n=16)
    E("sync", "wait_", w=(("st", V("st")),))
    return realize(nc, ops, semnames)


def realize(nc, ops, semnames):
    sems = {nm: nc.alloc_semaphore(name=nm) for nm in sorted(semnames)}

    names = {e: [] for e in ops}
    realize.NAMES = names

    def run(eng_obj, lst):
        eng_key = [k for k, v in ops.items() if v is lst][0]
        for method, args, kw, w, s, n in lst:
            for sn, v in w:
                if v > 0:
                    eng_obj.wait_ge(sems[sn], v)
            if method == "wait_":
                names[eng_key].append(None)
                continue
            ins = getattr(eng_obj, method)(*args, **kw)
            names[eng_key].append(ins.ins.name)
            if s is not None:
                ins.then_inc(sems[s], n)

    with nc.Block() as block:
        @block.tensor
        def _(t):
            run(t, ops["tensor"])

        @block.scalar
        def _(sc):
            run(sc, ops["scalar"])

        @block.vector
        def _(v):
            run(v, ops["vector"])

        @block.sync
        def _(sy):
            run(sy, ops["sync"])

        @block.gpsimd
        def _(g):
            run(g, ops["gpsimd"])
    return nc


def _prep_shared(i):
    f32 = np.float32
    import ml_dtypes
    bf = ml_dtypes.bfloat16
    C = np.ascontiguousarray

    def A(k):
        return np.asarray(i[k], f32)

    d = {}
    d["WiT"] = C(A("L0_attn_Wi").T)
    d["WsT"] = C(A("L0_attn_Ws").T)
    d["WoT"] = C(A("L0_attn_Wo").T)
    d["AT"] = C(A("L0_attn_A").T)
    d["BT"] = C(A("L0_attn_B").T)
    d["CT"] = C(A("L0_attn_C").T)
    d["bi"] = C(A("L0_attn_bi").reshape(128, 1))
    d["bs"] = C(A("L0_attn_bs").reshape(128, 1))
    d["bo"] = C(A("L0_attn_bo").reshape(8, 128).T)
    d["gwT"] = C(A("L0_ffn_gate_w").T)
    d["gateb"] = C(A("L0_ffn_gate_b").reshape(1, 8))
    d["w1"] = C(A("L0_ffn_w1"))
    d["w2"] = C(A("L0_ffn_w2").reshape(8, 2, 2, 8, 128, 8, 128)
                .transpose(0, 1, 5, 2, 4, 3, 6))
    d["b1c"] = C(A("L0_ffn_b1").reshape(8, 32, 128).transpose(2, 0, 1))
    d["b2c"] = C(A("L0_ffn_b2").reshape(8, 8, 128).transpose(2, 0, 1))
    d["WqT"] = C(A("L1_attn_Wq").T)
    d["WkT"] = C(A("L1_attn_Wk").T)
    d["WvT"] = C(A("L1_attn_Wv").T)
    d["WuT"] = C(A("L1_attn_Wu").T)
    d["qlw"] = C(np.tile(A("L1_attn_qln_w").reshape(64, 1), (2, 1)))
    d["qlb"] = C(np.tile(A("L1_attn_qln_b").reshape(64, 1), (2, 1)))
    d["klw"] = C(np.tile(A("L1_attn_kln_w").reshape(64, 1), (2, 1)))
    d["klb"] = C(np.tile(A("L1_attn_kln_b").reshape(64, 1), (2, 1)))
    import ml_dtypes as _mld
    sel = np.zeros((2, 128), _mld.bfloat16)
    sel[0, 0:64] = 1; sel[1, 64:128] = 1
    d["selm"] = sel
    d["mskb"] = C(sel.T)
    d["WrT"] = C(A("L1_ffn_W_r").T)
    d["WkfT"] = C(A("L1_ffn_W_k").T)
    d["WvfT"] = C(A("L1_ffn_W_v").T)
    d["mur"] = C(A("L1_ffn_mu_r").reshape(8, 128).T)
    d["muk"] = C(A("L1_ffn_mu_k").reshape(8, 128).T)
    q = np.arange(128)
    d["maskA"] = np.where(q[None, :] > q[:, None], np.float32(-1e30),
                          np.float32(0.0)).astype(f32)
    d["maskB"] = (q[:, None] <= q[None, :]).astype(bf)
    d["ident"] = np.eye(128, dtype=f32)
    return d


def kernel(**inputs):
    x = np.asarray(inputs["L0_attn_x"], np.float32)
    shared = _prep_shared(inputs)
    nc = build()
    in_maps = []
    for b in range(8):
        m = dict(shared)
        m["xT"] = np.ascontiguousarray(x[b].T)
        in_maps.append(m)
    res = run_bass_kernel_spmd(nc, in_maps, core_ids=list(range(8)))
    out = np.stack([np.ascontiguousarray(np.asarray(r["out"]).T) for r in res.results])
    return out.astype(np.float32)


# version 16 (final)
# speedup vs baseline: 1.0169x; 1.0169x over previous
import numpy as np
import concourse.bass as bass
import concourse.mybir as mybir
from concourse.bass_utils import run_bass_kernel_spmd

F32 = mybir.dt.float32
BF16 = mybir.dt.bfloat16
U8 = mybir.dt.uint8
AF = mybir.ActivationFunctionType
ALU = mybir.AluOpType
AX = mybir.AxisListType

T = 2048
D = 1024
NCH = 8
NE = 8
HEADS = 16
NT = 4
TS = 512
EPS = 1e-5

ENGS = ("tensor", "scalar", "vector", "sync", "gpsimd")

STAGE = 99


def build(stage=None):
    if stage is None:
        stage = STAGE
    nc = bass.Bass("TRN2", target_bir_lowering=False)

    def din(name, shape, dtype=F32):
        return nc.dram_tensor(name, list(shape), dtype, kind="ExternalInput").ap()

    xT = din("xT", [D, T])
    WiT_d = din("WiT", [D, 128]); WsT_d = din("WsT", [D, 128]); WoT_d = din("WoT", [128, D])
    AT_d = din("AT", [128, 128]); BT_d = din("BT", [128, 128]); CT_d = din("CT", [128, 128])
    bi_d = din("bi", [128, 1]); bs_d = din("bs", [128, 1]); bo_d = din("bo", [128, 8])
    gwT_d = din("gwT", [D, NE]); gb_d = din("gateb", [1, NE])
    w1_d = din("w1", [NE, D, 4096])
    w2_d = din("w2", [NE, 2, 8, 2, 128, 8, 128])
    b1c_d = din("b1c", [128, NE, 32]); b2c_d = din("b2c", [128, NE, 8])
    WqT_d = din("WqT", [D, D]); WkT_d = din("WkT", [D, D])
    WvT_d = din("WvT", [D, D]); WuT_d = din("WuT", [D, D])
    qlw_d = din("qlw", [128, 1]); qlb_d = din("qlb", [128, 1])
    klw_d = din("klw", [128, 1]); klb_d = din("klb", [128, 1])
    selm_d = din("selm", [2, 128], BF16)
    mskb_d = din("mskb", [128, 2], BF16)
    WrT_d = din("WrT", [D, D]); WkfT_d = din("WkfT", [D, D]); WvfT_d = din("WvfT", [D, D])
    mur_d = din("mur", [128, 8]); muk_d = din("muk", [128, 8])
    maskA_d = din("maskA", [128, 128])
    maskB_d = din("maskB", [128, 128], BF16)
    ident_d = din("ident", [128, 128])
    out_d = nc.dram_tensor("out", [D, T], F32, kind="ExternalOutput").ap()

    base = (nc.sbuf_base + 63) & ~63
    assert base + 212832 <= nc.sbuf_top + 32, (base, nc.sbuf_top)

    def place(name, shape, dtype, off):
        hnd = nc.alloc_sbuf_tensor_at(name, list(shape), dtype, offset=base + off)
        return hnd[:]

    A_OFF, B_OFF, S_OFF, W_OFF = 0, 65536, 131072, 163840
    ROW0, ROW1, G8_OFF, M_OFF = 186368, 194560, 202752, 210944

    xA = place("xA", [128, NCH, T], F32, A_OFF)
    xB = place("xB", [128, NCH, T], F32, B_OFF)
    xS = place("xS", [128, NCH, T], BF16, S_OFF)
    row0 = place("row0", [1, T], F32, ROW0)
    row1 = place("row1", [1, T], F32, ROW1)
    brow_m = place("brow_m", [2, T], BF16, ROW0)
    brow_r = place("brow_r", [2, T], BF16, ROW0 + 4096)
    brow_g = place("brow_g", [1, T], BF16, ROW1)
    rowT = place("rowT", [2, T], F32, ROW1)
    g8 = place("g8", [8, T], F32, G8_OFF)
    glall = place("glall", [128, 16, NE], F32, W_OFF + 8192)

    mo = [M_OFF]
    def mplace(name, shape, dtype):
        h = place(name, shape, dtype, mo[0])
        per = int(np.prod(shape[1:])) * mybir.dt.size(dtype)
        mo[0] += ((per + 63) // 64) * 64
        return h
    ones = mplace("ones", [128, 1], F32)
    ones_b = mplace("ones_b", [128, 1], BF16)
    ones_r = mplace("ones_r", [1, 128], F32)
    ones_rb = mplace("ones_rb", [1, 128], BF16)
    gwS = mplace("gwS", [128, NCH, NE], F32)
    biS = mplace("biS", [128, 1], F32)
    bsS = mplace("bsS", [128, 1], F32)
    boS = mplace("boS", [128, 8], F32)
    gbS = mplace("gbS", [1, NE], F32)
    murS = mplace("murS", [128, 8], F32)
    mukS = mplace("mukS", [128, 8], F32)
    qlwS = mplace("qlwS", [128, 1], F32); qlbS = mplace("qlbS", [128, 1], F32)
    klwS = mplace("klwS", [128, 1], F32); klbS = mplace("klbS", [128, 1], F32)
    qlw8, qlb8 = qlwS, qlbS
    selm = place("selm", [2, 128], BF16, W_OFF + 21760)
    mskb = place("mskb", [128, 2], BF16, W_OFF + 22016)
    assert mo[0] <= 212832, mo[0]

    state_bf = place("state_bf", [128, T], BF16, W_OFF)
    ys_bf = place("ys_bf", [128, T], BF16, W_OFF + 4096)
    WiB = place("WiB", [128, NCH, 128], BF16, W_OFF + 8192)
    WsB = place("WsB", [128, NCH, 128], BF16, W_OFF + 10240)
    WoB = place("WoB", [128, D], BF16, B_OFF + 29184)
    AT = place("AT_s", [128, 128], BF16, B_OFF + 28672)
    BT = place("BT_s", [128, 128], BF16, W_OFF + 14592)
    CT = place("CT_s", [128, 128], BF16, B_OFF + 28928)
    CT2 = place("CT2_s", [128, 128], BF16, A_OFF)
    WoB2 = place("WoB2", [128, D], BF16, A_OFF + 8192)
    ident = place("ident_s", [128, 128], F32, W_OFF + 17408)
    zsq0 = place("zsq0", [128, T], BF16, W_OFF)
    zsq1 = place("zsq1", [128, T], BF16, W_OFF + 4096)
    gscr = place("gscr", [128, 1024], F32, W_OFF)
    hbuf = place("hbuf", [128, 16, TS], BF16, W_OFF)
    b1c = place("b1c", [128, NE, 32], F32, W_OFF + 16384)
    b2c = place("b2c", [128, NE, 8], F32, W_OFF + 17920)
    g_sb = place("g_sb", [128, TS], F32, W_OFF + 18432)
    ytmp = place("ytmp", [128, TS], F32, W_OFF + 20480)
    wqb = [place(f"wqb{i}", [128, NCH, 128], BF16, W_OFF + i * 2048) for i in range(2)]
    wkb = [place(f"wkb{i}", [128, NCH, 128], BF16, W_OFF + 4096 + i * 2048) for i in range(2)]
    wvb = [place(f"wvb{i}", [128, NCH, 128], BF16, W_OFF + 8192 + i * 2048) for i in range(2)]
    wub = [[place(f"wub{i}_{h}", [64, D], BF16, W_OFF + 12288 + (i * 2 + h) * 2048)
            for h in range(2)] for i in range(2)]
    maskA = place("maskA_s", [128, 128], F32, W_OFF + 20480)
    maskB = place("maskB_s", [128, 128], BF16, W_OFF + 20992)
    identB = place("identB_s", [128, 128], F32, W_OFF + 21248)
    wfb = [place(f"wfb{i}", [128, NCH, 128], BF16, W_OFF + i * 2048) for i in range(2)]
    tsub = place("tsub", [128, NCH, TS], BF16, W_OFF + 4096)
    tmpk = place("tmpk", [128, TS], BF16, W_OFF + 12288)
    tmpv = place("tmpv", [128, TS], F32, W_OFF + 13312)

    u_bf = place("u_bf", [128, T], BF16, B_OFF)
    sp_n = [place("sp_n0", [128, TS], F32, B_OFF + 8192),
            place("sp_n1", [128, TS], F32, B_OFF + 4096),
            place("sp_n2", [128, TS], F32, B_OFF + 6144),
            place("sp_n3", [128, TS], F32, B_OFF + 45056)]
    c_n = [place("c_n0", [128, TS], F32, B_OFF + 16384),
           place("c_n1", [128, TS], F32, B_OFF + 12288),
           place("c_n2", [128, TS], F32, B_OFF + 14336),
           place("c_n3", [128, TS], F32, B_OFF + 36864)]
    tanh_bf = place("tanh_bf", [128, T], BF16, B_OFF + 20480)
    sp_bf = place("sp_bf", [128, T], BF16, B_OFF + 24576)

    w1b = place("w1b", [128, NCH, 2048], BF16, A_OFF)
    w2c = place("w2c", [128, NCH, 2, 8, 128], BF16, A_OFF + 32768)
    W1F_OFF = [(B_OFF + 51200, B_OFF + 53248), (B_OFF + 55296, B_OFF + 59392),
               (B_OFF + 61440, B_OFF + 63488), (B_OFF + 18432, B_OFF + 26624),
               (S_OFF + 1024, S_OFF + 5120), (S_OFF + 9216, S_OFF + 13312),
               (S_OFF + 17408, S_OFF + 21504), (S_OFF + 25600, S_OFF + 29696)]
    w1f = [[place(f"w1f{dk}_{j}", [128, 1024], BF16, W1F_OFF[dk][j])
            for j in range(2)] for dk in range(NCH)]

    Qh = [place(f"Qh{h}", [65, T], BF16, A_OFF + h * 4096) for h in range(2)]
    Kh = [place(f"Kh{h}", [65, T], BF16, A_OFF + 8192 + h * 4096) for h in range(2)]
    Vg = place("Vg", [128, 16, 2, 65], BF16, A_OFF + 16384)
    PT = place("PT", [128, T], BF16, A_OFF + 20608)
    PT2 = place("PT2", [128, T], BF16, A_OFF + 57856)
    qraw = place("qraw", [128, T], F32, A_OFF + 24704)
    qpb = place("qpb", [128, T], BF16, A_OFF + 32896)
    q2pb = place("q2pb", [128, T], BF16, A_OFF + 36992)
    Qpair = place("Qpair", [128, T], BF16, A_OFF + 41088)
    aout = [place(f"aout{h}", [64, T], BF16, A_OFF + 45184 + h * 4096) for h in range(2)]
    mcols = place("mcols", [128, 16], F32, A_OFF + 53376)
    tmp16 = place("tmp16", [16, 128], BF16, A_OFF + 53504)
    mtmp = place("mtmp", [128, 128], F32, A_OFF + 53760)
    mcol2 = place("mcol2", [128, 1], F32, A_OFF + 54336)
    xrk = place("xrk", [128, NCH, T], BF16, A_OFF)
    rF = place("rF", [128, NCH, T], BF16, A_OFF + 32768)
    kact = place("kact", [128, NCH, T], BF16, S_OFF)
    ost = place("ost", [128, NCH, T], F32, A_OFF)

    psA = nc.psum_tensor("psA", [128, 2048], F32).__enter__()[:]
    psB = nc.psum_tensor("psB", [128, 2048], F32).__enter__()[:]

    def bank(i):
        p = psA if i < 4 else psB
        j = i % 4
        return p[:, j * TS:(j + 1) * TS]

    ops = {e: [] for e in ENGS}
    cnt = {}
    semnames = set()

    last_inc = {"vector": None, "scalar": None, "gpsimd": None}

    def E(eng, method, *args, w=(), s=None, n=1, chain=True, **kw):
        w = tuple(w)
        if eng in last_inc and method != "wait_" and chain:
            if last_inc[eng] is not None:
                w = w + (last_inc[eng],)
            if s is None:
                s = "q" + eng[0]
        for sn, _v in w:
            semnames.add(sn)
        if s is not None:
            semnames.add(s)
            cnt[s] = cnt.get(s, 0) + n
            if eng in last_inc:
                last_inc[eng] = (s, cnt[s])
        ops[eng].append((method, args, kw, w, s, n))
        return cnt.get(s)

    def V(s):
        return cnt.get(s, 0)

    nbar = [0]
    PH = []

    def mark(name):
        PH.append((name, {e: len(ops[e]) for e in ENGS}))
    build.PHASES = PH

    def barrier():
        nbar[0] += 1
        tgt = 5 * nbar[0]
        for eng in ENGS:
            E(eng, "drain", s="bar")
            E(eng, "wait_", w=(("bar", tgt),))
        cnt["bar"] = tgt
        last_inc["vector"] = None
        last_inc["scalar"] = None
        last_inc["gpsimd"] = None

    def pb(ap, n=128):
        return ap.partition_broadcast(n)

    for c in range(NCH):
        E("sync", "dma_start", out=xA[:, c, :], in_=xT[c * 128:(c + 1) * 128, :],
          s="ldx", n=16)
    for dst, srcd in ((ident, ident_d), (biS, bi_d), (bsS, bs_d), (boS, bo_d),
                      (gbS, gb_d), (murS, mur_d), (mukS, muk_d),
                      (qlwS, qlw_d), (qlbS, qlb_d), (klwS, klw_d), (klbS, klb_d),

                      (b1c, b1c_d), (b2c, b2c_d)):
        E("sync", "dma_start", out=dst, in_=srcd, s="ldm", n=16)
    E("sync", "dma_start", out=gwS, in_=gwT_d.rearrange("(c p) e -> p c e", p=128),
      s="ldm", n=16)
    for dst, srcd in ((AT, AT_d), (BT, BT_d), (CT, CT_d), (WoB, WoT_d)):
        E("gpsimd", "dma_start", out=dst, in_=srcd, s="ldc", n=16)
    E("gpsimd", "dma_start", out=WiB, in_=WiT_d.rearrange("(c p) m -> p c m", p=128),
      s="ldc", n=16)
    E("gpsimd", "dma_start", out=WsB, in_=WsT_d.rearrange("(c p) m -> p c m", p=128),
      s="ldc", n=16)
    E("vector", "memset", ones, 1.0)
    E("vector", "memset", ones_b, 1.0)
    E("vector", "memset", ones_r, 1.0)
    E("vector", "memset", ones_rb, 1.0)
    ld_all = (("ldx", V("ldx")), ("ldm", V("ldm")), ("ldc", V("ldc")))
    E("vector", "tensor_copy", w=ld_all, out=xS, in_=xA, s="shad")
    E("vector", "tensor_scalar_mul", qlwS, qlwS, 0.125, w=(("ldm", V("ldm")),))
    E("vector", "tensor_scalar_mul", qlbS, qlbS, 0.125)
    for eng in ("tensor", "scalar", "sync", "gpsimd"):
        E(eng, "wait_", w=ld_all)
    barrier()

    mark("P1_usp")
    for nt in range(NT):
        b = nt % 2
        lo = nt * TS
        for dk in range(NCH):
            E("tensor", "matmul", bank(b), WiB[:, dk, :], xS[:, dk, lo:lo + TS],
              start=(dk == 0), stop=(dk == NCH - 1),
              w=((("evu", max(0, V("evu") - 1)),) if dk == 0 else ()),
              s=("mmu" if dk == NCH - 1 else None))
        E("vector", "tensor_scalar_add", u_bf[:, lo:lo + TS], bank(b), biS,
          w=(("mmu", V("mmu")),), s="evu")
    for nt in range(NT):
        b = 2 + nt % 2
        lo = nt * TS
        for dk in range(NCH):
            E("tensor", "matmul", bank(b), WsB[:, dk, :], xS[:, dk, lo:lo + TS],
              start=(dk == 0), stop=(dk == NCH - 1),
              w=((("evs", max(0, V("evs") - 1)),) if dk == 0 else ()),
              s=("mms" if dk == NCH - 1 else None))
        E("vector", "tensor_scalar_add", sp_n[nt], bank(b), bsS,
          w=(("mms", V("mms")),), s="evs")
    for nt in range(NT):
        E("vector", "tensor_copy", out=sp_bf[:, nt * TS:(nt + 1) * TS],
          in_=sp_n[nt],
          w=((("evs", V("evs")),) if nt == 0 else ()),
          s=("spb" if nt == NT - 1 else None))
    for nt in range(NT):
        b = 4 + nt % 2
        lo = nt * TS
        E("tensor", "matmul", bank(b), BT, u_bf[:, lo:lo + TS], start=True, stop=False,
          w=(("evu", 4), ("spb", V("spb")), ("evc", max(0, V("evc") - 1))))
        if nt == 0:
            E("tensor", "matmul", bank(b)[:, 1:TS], AT, sp_bf[:, 0:TS - 1],
              start=False, stop=True, s="mmc")
        else:
            E("tensor", "matmul", bank(b), AT, sp_bf[:, lo - 1:lo + TS - 1],
              start=False, stop=True, s="mmc")
        E("vector", "tensor_copy", out=c_n[nt], in_=bank(b),
          w=(("mmc", V("mmc")),), s="evc")
    barrier()

    def w1_load(e, half, w=()):
        for dk in range(NCH):
            E("gpsimd", "dma_start", out=w1b[:, dk, :],
              in_=w1_d[e, dk * 128:(dk + 1) * 128, half * 2048:(half + 1) * 2048],
              w=(w if dk == 0 else ()), s="w1l", n=16)

    def w1f_load(e, half, w=()):
        for dk in range(NCH):
            for j in range(2):
                c0 = half * 2048 + j * 1024
                E("gpsimd", "dma_start", out=w1f[dk][j],
                  in_=w1_d[e, dk * 128:(dk + 1) * 128, c0:c0 + 1024],
                  w=(w if (dk == 0 and j == 0) else ()), s="w1l", n=16)

    def w2_load(e, half, w=(), wo=None):
        done = []
        for o in range(NCH):
            wait = w if o == 0 else ()
            if wo is not None:
                wait = wait + (("mm2", wo[o]),)
            E("gpsimd", "dma_start", out=w2c[:, o],
              in_=w2_d[e, half, o].rearrange("j p h o -> p j h o"),
              w=wait, s="w2a", n=16)
            done.append(V("w2a"))
        return done

    HALVES = [(e, h) for e in range(NE) for h in range(2)]
    ssb = (psB[:, 0:1], psB[:, TS:TS + 1])

    def scan_step(t):
        if t == 0:
            E("scalar", "activation", tanh_bf[:, 0:1], c_n[0][:, 0:1], AF.Tanh,
              s="act", chain=False)
            return
        cbias = c_n[t // TS][:, t % TS:t % TS + 1]
        E("tensor", "matmul", ssb[t % 2], AT, tanh_bf[:, t - 1:t],
          start=True, stop=True, w=(("act", t),), s="spe")
        E("scalar", "activation", tanh_bf[:, t:t + 1], ssb[t % 2],
          AF.Tanh, bias=cbias, w=(("spe", t),), s="act", chain=False)

    CH_NTS = ((0,), (1, 2, 3))
    KA = 10
    mark("P2_scan")
    if stage >= 4:
        w1f_load(HALVES[0][0], HALVES[0][1])
    w1f_pre = V("w1l")
    for t in range(0, TS):
        scan_step(t)
    scan_cursor = [TS]

    def scan_pump(nsteps):
        t0 = scan_cursor[0]
        t1 = min(T, t0 + nsteps)
        for t in range(t0, t1):
            scan_step(t)
        scan_cursor[0] = t1

    def l0out(ch):
        nts = CH_NTS[ch]
        CTu = CT if ch == 0 else CT2
        WoBu = WoB if ch == 0 else WoB2
        for i, nt in enumerate(nts):
            lo = nt * TS
            E("vector", "tensor_add", state_bf[:, lo:lo + TS],
              tanh_bf[:, lo:lo + TS], sp_n[nt],
              w=((("act", (nts[-1] + 1) * TS),) if i == 0 else ()), s="stb")
        stb_t = V("stb")
        for i, nt in enumerate(nts):
            b = i % 2
            lo = nt * TS
            E("tensor", "matmul", bank(b), CTu, state_bf[:, lo:lo + TS],
              start=True, stop=True,
              w=(("stb", stb_t), ("evy", max(0, V("evy") - 1)),
                 ("ldc2", V("ldc2"))), s="mmy")
            E("scalar", "activation", ys_bf[:, lo:lo + TS], bank(b), AF.Copy,
              w=(("mmy", V("mmy")),), s="evy")
        evy_t = V("evy")
        for o in range(NCH):
            for i, nt in enumerate(nts):
                b = 2 + (o * len(nts) + i) % 2
                lo = nt * TS
                E("tensor", "matmul", bank(b), WoBu[:, o * 128:(o + 1) * 128],
                  ys_bf[:, lo:lo + TS], start=True, stop=True,
                  w=(("evy", evy_t), ("evz", max(0, V("evz") - 1))), s="mmz")
                E("vector", "scalar_tensor_tensor", xB[:, o, lo:lo + TS], bank(b),
                  boS[:, o:o + 1], xA[:, o, lo:lo + TS], ALU.add, ALU.add,
                  w=(("mmz", V("mmz")), ("ldx2", V("ldx2"))), s="evz")
        E("vector", "drain", s="l0d")

    def emit_ln(src_t, dst, shadow, tag, nts=(0, 1, 2, 3), after=()):
        zs = (zsq0, zsq1)
        r0 = nts[0] * TS
        r1 = (nts[-1] + 1) * TS
        nchunk = r1 - r0
        for c in range(NCH):
            E("scalar", "activation", zs[c % 2][:, r0:r1], src_t[:, c, r0:r1],
              AF.Square,
              w=(("zrd", max(0, V("zrd") - 1)),) + (tuple(after) if c == 0 else ()),
              s="sq")
            for nt in nts:
                lo = nt * TS
                E("tensor", "matmul", psA[0:1, lo:lo + TS], ones,
                  src_t[:, c, lo:lo + TS], start=(c == 0), stop=(c == NCH - 1),
                  w=(tuple(after) if (c == 0 and nt == nts[0]) else ()))
                E("tensor", "matmul", psB[0:1, lo:lo + TS], ones_b,
                  zs[c % 2][:, lo:lo + TS], start=(c == 0), stop=(c == NCH - 1),
                  w=((("sq", V("sq")),) if nt == nts[0] else ()),
                  s=("zrd" if nt == nts[-1] else None))
        rowS = g8[0:1, r0:r1]
        E("vector", "tensor_scalar_mul", row1[:, r0:r1], psA[0:1, r0:r1], 1.0 / D,
          w=(("zrd", V("zrd")),))
        E("vector", "tensor_scalar_mul", row0[:, r0:r1], psB[0:1, r0:r1], 1.0 / D)
        E("vector", "tensor_tensor", rowS, row1[:, r0:r1], row1[:, r0:r1], ALU.mult)
        E("vector", "tensor_sub", row0[:, r0:r1], row0[:, r0:r1], rowS)
        E("vector", "tensor_scalar_add", row0[:, r0:r1], row0[:, r0:r1], EPS, s="vr")
        E("scalar", "activation", rowS, row0[:, r0:r1], AF.Ln,
          w=(("vr", V("vr")),))
        E("scalar", "activation", row0[:, r0:r1], rowS, AF.Exp, scale=-0.5,
          s="rs")
        E("vector", "tensor_tensor", row1[:, r0:r1], row1[:, r0:r1], row0[:, r0:r1],
          ALU.mult, w=(("rs", V("rs")),), s="lnw")
        lnw = V("lnw")
        rsc = ((ytmp, g_sb), (gscr[:, 0:TS], gscr[:, TS:2 * TS]))
        for nt in nts:
            lo = nt * TS
            q = 4 + 2 * (nt % 2)
            E("tensor", "matmul", bank(q), ones_r, row0[:, lo:lo + TS],
              start=True, stop=True,
              w=(("lnw", lnw), ("apl", max(0, V("apl") - 1))))
            E("tensor", "matmul", bank(q + 1), ones_r, row1[:, lo:lo + TS],
              start=True, stop=True, s="bcl")
            rA, rB = rsc[nt % 2]
            E("scalar", "activation", rA, bank(q), AF.Copy,
              w=(("bcl", V("bcl")), ("apl", max(0, V("apl") - 1))), s="bcs")
            E("scalar", "activation", rB, bank(q + 1), AF.Copy, s="bcs")
            for c in range(NCH):
                E("vector", "tensor_tensor", dst[:, c, lo:lo + TS],
                  src_t[:, c, lo:lo + TS], rA, ALU.mult,
                  w=((("bcs", V("bcs")),) if c == 0 else ()))
                E("vector", "tensor_tensor", dst[:, c, lo:lo + TS],
                  dst[:, c, lo:lo + TS], rB, ALU.subtract)
                if shadow is not None:
                    E("vector", "tensor_copy", out=shadow[:, c, lo:lo + TS],
                      in_=dst[:, c, lo:lo + TS],
                      s=("apl" if c == NCH - 1 else None))
                elif c == NCH - 1:
                    E("vector", "drain", s="apl")
        E("vector", "drain", s="lnd")

    def gate(ch):
        nts = CH_NTS[ch]
        tt0 = nts[0] * 4
        ntt = len(nts) * 4
        for tt in range(tt0, tt0 + ntt):
            b = 4 + tt % 2
            pt = bank(b)[:, 0:NE]
            lo = tt * 128
            for dk in range(NCH):
                E("tensor", "matmul", pt, xA[:, dk, lo:lo + 128], gwS[:, dk, :],
                  start=(dk == 0), stop=False,
                  w=((("ggl", max(0, V("ggl") - 1)), ("apl", V("apl")))
                     if dk == 0 else ()))
            E("tensor", "matmul", pt, ones_r, gbS, start=False, stop=True, s="gmm")
            E("vector", "tensor_copy", out=glall[:, tt, :], in_=pt,
              w=(("gmm", V("gmm")),), s="ggl")
        gp3 = gscr[:, 0:8 * ntt].rearrange("p (t e) -> p t e", t=ntt)
        gm3 = gscr[:, 512:512 + 8 * ntt].rearrange("p (t e) -> p t e", t=ntt)
        gx = gscr[:, 256:256 + ntt].rearrange("p (t o) -> p t o", o=1)
        gv = gscr[:, 768:768 + ntt].rearrange("p (t o) -> p t o", o=1)
        gsl = glall[:, tt0:tt0 + ntt, :]

        def bc8(ap):
            return ap.to_broadcast([128, ntt, NE])

        E("vector", "reduce_max", gx, gsl, AX.X, s="gmx")
        E("vector", "tensor_tensor", gp3, gsl, bc8(gx), ALU.subtract)
        E("scalar", "activation", gp3, gp3, AF.Exp, w=(("qv", V("qv")),), s="gex")
        E("vector", "reduce_sum", gv, gp3, AX.X, w=(("gex", V("gex")),))
        E("vector", "reciprocal", gx, gv)
        E("vector", "tensor_tensor", gp3, gp3, bc8(gx), ALU.mult)
        E("vector", "reduce_max", gv, gp3, AX.X)
        E("vector", "tensor_tensor", gm3, gp3, bc8(gv), ALU.is_equal)
        E("vector", "scalar_tensor_tensor", gm3, gm3, -1e30, gp3, ALU.mult, ALU.add)
        E("vector", "reduce_max", gv, gm3, AX.X)
        E("vector", "tensor_tensor", gm3, gp3, bc8(gv), ALU.is_ge)
        E("vector", "tensor_mul", gsl, gp3, gm3, s="gtile")
        for tt in range(tt0, tt0 + ntt):
            lo = tt * 128
            E("tensor", "transpose", psB[0:NE, 1024:1152], glall[:, tt, :], ident,
              w=(("gtile", V("gtile")), ("gev", V("gev"))), s="gtr")
            E("vector", "tensor_copy", out=g8[:, lo:lo + 128],
              in_=psB[0:NE, 1024:1152],
              w=(("gtr", V("gtr")),), s="gev")
        E("vector", "tensor_copy",
          out=xB[:, :, nts[0] * TS:(nts[-1] + 1) * TS],
          in_=xA[:, :, nts[0] * TS:(nts[-1] + 1) * TS], s="zin")


    def moe(nts, halves, pp, hook=None, pre=None):
        w1done = {}
        zw = (("zin", V("zin")),)
        if V("mm1") > 0:
            zw = zw + (("mm1", V("mm1")),)
        if pre is None:
            w1_load(halves[0][0], halves[0][1], w=zw)
            w1done[0] = V("w1l")
        else:
            w1done[0] = pre
        mm2_slice = None
        for hi, (e, half) in enumerate(halves):
            if hi == 0:
                w2w = zw + ((("mm2", V("mm2")),) if V("mm2") > 0 else ())
                w2done_h = w2_load(e, half, w=w2w)
            else:
                w2done_h = w2_load(e, half, wo=mm2_slice)
            mm2_slice = [0] * NCH
            if pp and hi + 1 < len(halves):
                e2, h2 = halves[hi + 1]
                wn = zw if hi == 0 else (("mm1", V("mm1")),)
                if (hi + 1) % 2 == (0 if pre is None else 1):
                    w1_load(e2, h2, w=wn)
                else:
                    w1f_load(e2, h2, w=wn)
                w1done[hi + 1] = V("w1l")
            w1_tgt = w1done[hi] if pp else V("w1l")
            if half == 0:
                cl = nts[0] * TS
                cw = len(nts) * TS
                E("sync", "dma_start", out=row0[:, cl:cl + cw],
                  in_=g8[e:e + 1, cl:cl + cw],
                  w=(("gbc", V("gbc")),), s="grw", n=16)
                E("vector", "tensor_copy", out=brow_g[:, cl:cl + cw],
                  in_=row0[:, cl:cl + cw],
                  w=(("grw", V("grw")), ("gbc", V("gbc"))), s="gcv")
            gcv_t = V("gcv")
            for nt in nts:
                lo = nt * TS
                gq = 6 + nt % 2
                E("tensor", "matmul", bank(gq), ones_rb, brow_g[:, lo:lo + TS],
                  start=True, stop=True,
                  w=(("gsc", max(0, V("gsc") - 1)), ("gcv", gcv_t)), s="gbc")
                E("scalar", "activation", g_sb, bank(gq), AF.Copy,
                  w=(("gbc", V("gbc")), ("ye", V("ye"))), s="gsc")
                gsb_tgt = V("gsc")
                frag = pp and hi % 2 == (1 if pre is None else 0)
                for hk in range(16):
                    b = hk % 2
                    for dk in range(NCH):
                        hsem = "hes" if hk % 2 == 0 else "hev"
                        wt1 = (w1f[dk][hk // 8][:, (hk % 8) * 128:
                                                (hk % 8 + 1) * 128] if frag
                               else w1b[:, dk, hk * 128:(hk + 1) * 128])
                        E("tensor", "matmul", bank(b), wt1,
                          xS[:, dk, lo:lo + TS], start=(dk == 0),
                          stop=(dk == NCH - 1),
                          w=((("w1l", w1_tgt), (hsem, V(hsem)))
                             if dk == 0 else ()),
                          s=("mm1" if dk == NCH - 1 else None))
                        if hook is not None:
                            hook()
                    b1col = b1c[:, e, half * 16 + hk:half * 16 + hk + 1]
                    if hk % 2 == 0:
                        E("scalar", "activation", hbuf[:, hk, :], bank(b), AF.Relu,
                          bias=b1col, w=(("mm1", V("mm1")), ("mm2", V("mm2"))),
                          s="hes")
                    else:
                        E("vector", "tensor_scalar", hbuf[:, hk, :], bank(b), b1col,
                          0.0, ALU.add, ALU.max,
                          w=(("mm1", V("mm1")), ("mm2", V("mm2"))), s="hev")
                    if hook is not None:
                        hook()
                he_tgt = (("hes", V("hes")), ("hev", V("hev")))
                for o in range(NCH):
                    b = 2 + o % 2
                    w2s = ("w2a", w2done_h[o])
                    for hk in range(16):
                        E("tensor", "matmul", bank(b),
                          w2c[:, o, hk // 8, hk % 8, :],
                          hbuf[:, hk, :], start=(hk == 0), stop=(hk == 15),
                          w=(((w2s, ("ye", max(0, V("ye") - 1)))
                              + he_tgt) if hk == 0 else ()),
                          s=("mm2" if hk == 15 else None))
                    mm2_slice[o] = V("mm2")
                    b2col = b2c[:, e, o:o + 1] if half == 0 else 0.0
                    E("vector", "scalar_tensor_tensor", ytmp, bank(b), b2col,
                      g_sb, ALU.add, ALU.mult,
                      w=(("mm2", V("mm2")), ("gsc", gsb_tgt)))
                    E("vector", "tensor_add", xB[:, o, lo:lo + TS],
                      xB[:, o, lo:lo + TS], ytmp, s="ye")
                    if hook is not None:
                        hook()
            if not pp and hi + 1 < len(halves):
                e2, h2 = halves[hi + 1]
                w1_load(e2, h2, w=(("mm1", V("mm1")),))

    if stage < 4:
        scan_pump(T - TS)
    for ch in (0, 1):
        if ch == 1 and stage >= 4:
            for c in range(NCH):
                E("sync", "dma_start", out=xA[:, c, TS:T],
                  in_=xT[c * 128:(c + 1) * 128, TS:T], s="ldx2", n=16)
            E("gpsimd", "dma_start", out=CT2, in_=CT_d, s="ldc2", n=16)
            E("gpsimd", "dma_start", out=WoB2, in_=WoT_d, s="ldc2", n=16)
        mark(f"l0out_c{ch}")
        l0out(ch)
        mark(f"ln1_c{ch}")
        emit_ln(xB, xA, xS, "l1", nts=CH_NTS[ch],
                after=(("l0d", V("l0d")),))
        mark(f"gate_c{ch}")
        gate(ch)
        if stage >= 4:
            barrier()
            if ch == 0:
                mark("moe_c0")
                nhooks = KA * (16 * 9 + 8)
                per = [0]

                def hook(per=per):
                    per[0] += 1
                    tgt = TS + ((T - TS) * per[0]) // nhooks
                    scan_pump(tgt - scan_cursor[0])
                moe(CH_NTS[0], HALVES[:KA], pp=True, hook=hook, pre=w1f_pre)
                scan_pump(T - scan_cursor[0])
            else:
                mark("moe_mix")
                moe((0, 1, 2, 3), HALVES[KA:], pp=False)
                mark("moe_c1")
                moe(CH_NTS[1], HALVES[:KA], pp=False)
            barrier()
    if stage == 1:
        return finish(nc, E, V, ops, semnames, xB, out_d, NCH)
    if stage in (2, 3):
        return finish(nc, E, V, ops, semnames, xA, out_d, NCH)
    mark("LN2")
    emit_ln(xB, xA, xS, "l2")
    barrier()
    if stage == 4:
        return finish(nc, E, V, ops, semnames, xA, out_d, NCH)

    mark("P8_attn")
    E("vector", "tensor_copy", out=xB, in_=xA, s="zpin")
    E("sync", "dma_start", out=maskA, in_=maskA_d, s="lda", n=16)
    E("sync", "dma_start", out=maskB, in_=maskB_d, s="lda", n=16)
    E("sync", "dma_start", out=identB, in_=ident_d, s="lda", n=16)
    E("sync", "dma_start", out=selm, in_=selm_d, s="lda", n=16)
    E("sync", "dma_start", out=mskb, in_=mskb_d, s="lda", n=16)
    lda_t = V("lda")
    barrier()

    def attw_load(g):
        p = g % 2
        wt = (("wmm", max(0, (g - 1) * 32)),)
        first = True
        for mat_d, dstb in ((WqT_d, wqb[p]), (WkT_d, wkb[p]), (WvT_d, wvb[p])):
            for dh in range(2):
                E("gpsimd", "dma_start",
                  out=dstb[:, dh * 4:(dh + 1) * 4, :],
                  in_=mat_d[dh * 512:(dh + 1) * 512, g * 128:(g + 1) * 128]
                  .rearrange("(dk p) m -> p dk m", p=128),
                  w=(wt if first else ()), s=f"awl{p}", n=16)
                first = False
        for h in range(2):
            E("gpsimd", "dma_start", out=wub[p][h],
              in_=WuT_d[g * 128 + h * 64:g * 128 + (h + 1) * 64, :],
              s=f"awl{p}", n=16)

    attw_load(0)
    attw_load(1)
    AWL_PER_G = 8 * 16

    for g in range(8):
        p = g % 2
        awl_w = (f"awl{g % 2}", AWL_PER_G * (g // 2 + 1))
        mark(f"attn_heads_g{g}")
        mark(f"g{g}_proj")
        qev_base = V("qev")
        for which in ("q", "k"):
            wbuf = wqb[p] if which == "q" else wkb[p]
            for nt in range(NT):
                lo = nt * TS
                prev_evac = qev_base + nt + 1 - (4 if which == "q" else 0)
                for dk in range(NCH):
                    E("tensor", "matmul", bank(nt), wbuf[:, dk, :],
                      xS[:, dk, lo:lo + TS],
                      start=(dk == 0), stop=(dk == NCH - 1),
                      w=((("qev", max(0, prev_evac)), ("vev", V("vev")),
                          ("aev", V("aev")), awl_w,
                          ("wad", max(0, V("wad") - 3 + nt)))
                         if dk == 0 else ()),
                      s=("qmm" if dk == NCH - 1 else None))
        E("vector", "memset", Vg[:, :, :, 64:65], 1.0,
          w=(("amm", V("amm")),), s="vms")
        for tt in range(16):
            b = 4 + tt % 2
            for dk in range(NCH):
                E("tensor", "matmul", bank(b)[:, 0:128],
                  xS[:, dk, tt * 128:(tt + 1) * 128], wvb[p][:, dk, :],
                  start=(dk == 0), stop=(dk == NCH - 1),
                  w=((awl_w, ("vev", max(0, V("vev") - 1)))
                     if dk == 0 else ()),
                  s=("vmm" if dk == NCH - 1 else None))
            E("vector", "tensor_copy",
              out=Vg[:, tt, :, 0:64],
              in_=bank(b)[:, 0:128].rearrange("p (h s) -> p h s", h=2),
              w=(("vmm", V("vmm")), ("vms", V("vms"))), s="vev")
        for which in ("q", "k"):
            dst0 = Qh if which == "q" else Kh
            wcol = qlw8 if which == "q" else klwS
            bcol = qlb8 if which == "q" else klbS
            qmm_t = V("qmm") - (NT if which == "q" else 0)
            for nt in range(NT):
                lo = nt * TS
                E("scalar", "activation", qraw[:, lo:lo + TS], bank(nt),
                  AF.Copy, w=(("qmm", qmm_t - NT + nt + 1), ("qcb", V("qcb")),
                              ("hvr", V("hvr"))), s="qev")
            E("scalar", "activation", q2pb, qraw, AF.Square,
              w=(("qev", V("qev")), ("smm", V("smm"))), s="sq2")
            E("vector", "tensor_copy", out=qpb, in_=qraw,
              w=(("qev", V("qev")), ("smm", V("smm"))), s="qcb")
            for nt in range(NT):
                lo = nt * TS
                E("tensor", "matmul", psB[0:2, lo:lo + TS], mskb,
                  qpb[:, lo:lo + TS], start=True, stop=True,
                  w=((("qcb", V("qcb")),
                      ("hvm", V("hvm")), ("lda", lda_t),
                      ("vev", V("vev")), ("aev", V("aev"))) if nt == 0 else ()),
                  s=("smm" if nt == NT - 1 else None))
            E("vector", "tensor_scalar_mul", rowT, psB[0:2, :], 1.0 / 64,
              w=(("smm", V("smm")), ("hrs", V("hrs")), ("rlw", V("rlw"))),
              s="hvm")
            for nt in range(NT):
                lo = nt * TS
                E("tensor", "matmul", psB[0:2, lo:lo + TS], mskb,
                  q2pb[:, lo:lo + TS], start=True, stop=True,
                  w=((("sq2", V("sq2")), ("hvm", V("hvm"))) if nt == 0 else ()),
                  s=("smm" if nt == NT - 1 else None))
            rowS = qraw[0:2, :]
            E("vector", "tensor_scalar_mul", rowS, psB[0:2, :], 1.0 / 64,
              w=(("smm", V("smm")),), s="hvm")
            E("vector", "tensor_copy", out=brow_m, in_=rowT,
              w=(("hbc", V("hbc")), ("rlb", V("rlb"))), s="hmw")
            E("vector", "tensor_mul", rowT, rowT, rowT)
            E("vector", "tensor_sub", rowS, rowS, rowT)
            E("vector", "tensor_scalar_add", rowS, rowS, EPS, s="hvr")
            E("scalar", "activation", rowT, rowS, AF.Ln,
              w=(("hvr", V("hvr")),))
            E("scalar", "activation", brow_r, rowT, AF.Exp,
              scale=-0.5, s="hrs")
            hlw_t = V("hrs")
            for nt in range(NT):
                lo = nt * TS
                q2 = 4 + 2 * (nt % 2)
                E("tensor", "matmul", bank(q2), selm, brow_r[:, lo:lo + TS],
                  start=True, stop=True,
                  w=(("hrs", hlw_t), ("hmw", V("hmw")), ("lda", lda_t),
                     ("hap", max(0, V("hap") - 1))))
                E("tensor", "matmul", bank(q2 + 1), selm, brow_m[:, lo:lo + TS],
                  start=True, stop=True, s="hbc")
                E("vector", "tensor_tensor", Qpair[:, lo:lo + TS],
                  qpb[:, lo:lo + TS], bank(q2 + 1), ALU.subtract,
                  w=(("hbc", V("hbc")), ("qdl", V("qdl"))))
                E("vector", "tensor_tensor", Qpair[:, lo:lo + TS],
                  Qpair[:, lo:lo + TS], bank(q2), ALU.mult, s="hap")
            E("vector", "tensor_scalar", Qpair, Qpair, wcol, bcol,
              ALU.mult, ALU.add, w=(("mB", V("mB")), ("hap", V("hap"))),
              s="qap")
            for hl2 in range(2):
                E("sync", "dma_start", out=dst0[hl2][0:64, :],
                  in_=Qpair[hl2 * 64:(hl2 + 1) * 64, :],
                  w=((("qap", V("qap")), ("mB", V("mB"))) if hl2 == 0 else ()),
                  s="qdl", n=16)
            if which == "k":
                E("vector", "memset", Kh[0][64:65, :], 1.0, s="kap")
                E("vector", "memset", Kh[1][64:65, :], 1.0, s="kap")
        qdl_t = V("qdl")
        for hl in range(2):
            h = 2 * g + hl
            mark(f"h{g}_{hl}_passA")
            pass
            for qt in range(16):
                kext = (qt + 1) * 128
                ps = psA if qt % 2 == 0 else psB
                nsub = (kext + TS - 1) // TS
                for j in range(nsub):
                    n0 = j * TS
                    n1 = min(kext, n0 + TS)
                    E("tensor", "matmul", ps[:, n0:n1],
                      Qh[hl][0:64, qt * 128:(qt + 1) * 128], Kh[hl][0:64, n0:n1],
                      start=True, stop=True,
                      w=((("qdl", qdl_t), ("lda", lda_t),
                          ("mxA", max(0, V("mxA") - 1)),
                          ("aev", V("aev"))) if j == 0 else ()),
                      s=("mA" if j == nsub - 1 else None))
                E("vector", "tensor_add", mtmp, ps[:, qt * 128:kext],
                  maskA, w=(("mA", V("mA")),))
                if qt == 0:
                    E("vector", "reduce_max", mcols[:, 0:1], mtmp, AX.X, s="mxA")
                else:
                    E("vector", "reduce_max", mcols[:, qt:qt + 1],
                      ps[:, 0:qt * 128], AX.X)
                    E("vector", "reduce_max", mcol2, mtmp, AX.X)
                    E("vector", "tensor_max", mcols[:, qt:qt + 1],
                      mcols[:, qt:qt + 1], mcol2, s="mxA")
            mark(f"h{g}_{hl}_mrow")
            E("tensor", "transpose", psA[0:16, 0:128], mcols, identB,
              w=(("mxA", V("mxA")),), s="mtr")
            E("vector", "tensor_scalar_mul", tmp16, psA[0:16, 0:128], -1.0,
              w=(("mtr", V("mtr")),), s="mng")
            E("sync", "dma_start", out=Qh[hl][64:65, :], in_=tmp16,
              w=(("mng", V("mng")),), s="mdl", n=16)
            mark(f"h{g}_{hl}_passB")
            mdl_t = V("mdl")
            expcnt = {0: V("xpb"), 1: V("xpb")}
            jexp_all = {}
            ptm_all = {}

            def emit_attn(kt):
                q0 = kt * 128
                PTb = PT if kt % 2 == 0 else PT2
                for j in range(kt // 4, 4):
                    s0 = max(q0, j * TS)
                    s1 = (j + 1) * TS
                    ww = [("xpb", jexp_all[(kt, j)])]
                    if j == kt // 4:
                        ww.append(("ptm", ptm_all[kt]))
                    if kt == 0:
                        ww.append(("vev", V("vev")))
                        ww.append(("aev", V("aev")))
                    E("tensor", "matmul", psB[0:65, s0:s1], Vg[:, kt, hl, :],
                      PTb[:, s0:s1], start=(kt == 0), stop=(kt == 4 * j + 3),
                      w=tuple(ww), s=("amm" if j == 3 else None))

            for kt in range(16):
                q0 = kt * 128
                first_j = kt // 4
                jlist = list(range(first_j, 4))
                PTb = PT if kt % 2 == 0 else PT2
                segs = []
                if kt < 8:
                    ja = [j for j in jlist if j < 2]
                    jb = [j for j in jlist if j >= 2]
                    if ja:
                        segs.append((0, max(q0, ja[0] * TS), 1024, ja))
                    segs.append((1, max(q0, jb[0] * TS), 2048, jb))
                else:
                    grp = kt % 2
                    for j in jlist:
                        segs.append((grp, max(q0, j * TS), (j + 1) * TS, [j]))
                for grp, c0, c1, js in segs:
                    for idx, j in enumerate(js):
                        s0 = max(q0, j * TS)
                        s1 = (j + 1) * TS
                        p0 = s0 if kt < 8 else grp * 1024 + (s0 - c0)
                        E("tensor", "matmul", psA[:, p0:p0 + (s1 - s0)],
                          Kh[hl][0:65, q0:q0 + 128], Qh[hl][0:65, s0:s1],
                          start=True, stop=True,
                          w=((("mdl", mdl_t), ("xpb", expcnt[grp]),
                              ("qdl", qdl_t), ("kap", V("kap")))
                             if idx == 0 else ()),
                          s="mB")
                    p0 = c0 if kt < 8 else grp * 1024
                    E("scalar", "activation", PTb[:, c0:c1],
                      psA[:, p0:p0 + (c1 - c0)], AF.Exp,
                      w=(("mB", V("mB")),), s="xpb")
                    expcnt[grp] = V("xpb")
                    for j in js:
                        jexp_all[(kt, j)] = V("xpb")
                    if c0 <= q0 < c1:
                        E("vector", "tensor_mul", PTb[:, q0:q0 + 128],
                          PTb[:, q0:q0 + 128], maskB,
                          w=(("xpb", V("xpb")),), s="ptm")
                        ptm_all[kt] = V("ptm")
                if kt > 0:
                    emit_attn(kt - 1)
            emit_attn(15)
            mark(f"h{g}_{hl}_evac")
            amm_t = V("amm")
            E("scalar", "activation", rowT[0:1, :], psB[64:65, :], AF.Ln,
              w=(("amm", amm_t),))
            E("scalar", "activation", brow_m[0:1, :], rowT[0:1, :], AF.Exp,
              scale=-1.0, s="rlw")
            for j in range(4):
                c0 = j * TS
                E("vector", "tensor_copy", out=aout[hl][:, c0:c0 + TS],
                  in_=psB[0:64, c0:c0 + TS],
                  w=((("amm", amm_t),) if j == 0 else ()),
                  s=("acp" if j == 3 else None))
            for j in range(4):
                c0 = j * TS
                E("tensor", "matmul", psA[:, c0:c0 + TS], ones_rb,
                  brow_m[0:1, c0:c0 + TS], start=True, stop=True,
                  w=((("rlw", V("rlw")),) if j == 0 else ()),
                  s=("rlb" if j == 3 else None))
            E("vector", "tensor_mul", aout[hl], aout[hl], psA[0:64, :],
              w=(("rlb", V("rlb")), ("acp", V("acp"))), s="aev")
        mark(f"g{g}_wu")
        aev_t = V("aev")
        for o in range(NCH):
            for nt in range(NT):
                b = (o * NT + nt) % 4
                lo = nt * TS
                E("tensor", "matmul", bank(b), wub[p][0][:, o * 128:(o + 1) * 128],
                  aout[0][:, lo:lo + TS], start=True, stop=False,
                  w=(("aev", aev_t), ("wad", max(0, V("wad") - 3))))
                E("tensor", "matmul", bank(b), wub[p][1][:, o * 128:(o + 1) * 128],
                  aout[1][:, lo:lo + TS], start=False, stop=True, s="wmm")
                E("vector", "tensor_add", xB[:, o, lo:lo + TS], xB[:, o, lo:lo + TS],
                  bank(b), w=(("wmm", V("wmm")),), s="wad")
        if g + 2 < 8:
            attw_load(g + 2)
    barrier()
    emit_ln(xB, xA, xS, "l3")
    barrier()
    if stage == 5:
        return finish(nc, E, V, ops, semnames, xA, out_d, NCH)

    mark("P10_ffn")
    E("vector", "tensor_copy", out=xB, in_=xA, s="z2in")

    def mix(dst_name, muS, sem, w0=()):
        for nt in range(NT):
            lo = nt * TS
            for c in range(NCH):
                if nt == 0:
                    E("vector", "tensor_sub", tsub[:, c, 1:TS],
                      xS[:, c, 1:TS], xS[:, c, 0:TS - 1],
                      w=(w0 if c == 0 else ()))
                    E("vector", "scalar_tensor_tensor", xrk[:, c, lo + 1:lo + TS],
                      tsub[:, c, 1:TS], muS[:, c:c + 1], xS[:, c, 0:TS - 1],
                      ALU.mult, ALU.add)
                    E("vector", "tensor_single_scalar", xrk[:, c, 0:1],
                      xS[:, c, 0:1], muS[:, c:c + 1], ALU.mult)
                else:
                    E("vector", "tensor_sub", tsub[:, c, :],
                      xS[:, c, lo:lo + TS], xS[:, c, lo - 1:lo + TS - 1])
                    E("vector", "scalar_tensor_tensor", xrk[:, c, lo:lo + TS],
                      tsub[:, c, :], muS[:, c:c + 1], xS[:, c, lo - 1:lo + TS - 1],
                      ALU.mult, ALU.add,
                      s=(sem if (nt == NT - 1 and c == NCH - 1) else None))

    def ffw_load(mat_d, o, w=()):
        E("gpsimd", "dma_start", out=wfb[o % 2],
          in_=mat_d[:, o * 128:(o + 1) * 128].rearrange("(dk p) m -> p dk m", p=128),
          w=w, s=f"fwl{o % 2}", n=16)

    mix("r", murS, "xrd")
    ffw_load(WrT_d, 0)
    ffw_load(WrT_d, 1)
    for o in range(NCH):
        fw_w = (f"fwl{o % 2}", 16 * (0 * 4 + o // 2 + 1))
        for nt in range(NT):
            b = nt % 2
            lo = nt * TS
            for dk in range(NCH):
                E("tensor", "matmul", bank(b), wfb[o % 2][:, dk, :],
                  xrk[:, dk, lo:lo + TS], start=(dk == 0), stop=(dk == NCH - 1),
                  w=((fw_w, ("xrd", 1),
                      ("rev", max(0, V("rev") - 1))) if dk == 0 else ()),
                  s=("rmm" if dk == NCH - 1 else None))
            E("scalar", "activation", rF[:, o, lo:lo + TS], bank(b), AF.Sigmoid,
              w=(("rmm", V("rmm")),), s="rev")
        if o + 2 < NCH:
            ffw_load(WrT_d, o + 2, w=(("rmm", V("rmm")),))
        elif o == NCH - 2:
            ffw_load(WkfT_d, 0, w=(("rmm", V("rmm")),))
        elif o == NCH - 1:
            ffw_load(WkfT_d, 1, w=(("rmm", V("rmm")),))
    mix("k", mukS, "xkd", w0=(("rmm", 32),))
    for o in range(NCH):
        fw_w = (f"fwl{o % 2}", 16 * (1 * 4 + o // 2 + 1))
        for nt in range(NT):
            b = nt % 2
            lo = nt * TS
            for dk in range(NCH):
                E("tensor", "matmul", bank(b), wfb[o % 2][:, dk, :],
                  xrk[:, dk, lo:lo + TS], start=(dk == 0), stop=(dk == NCH - 1),
                  w=((fw_w, ("xkd", 1), ("rev", V("rev")),
                      ("kev", max(0, V("kev") - 1))) if dk == 0 else ()),
                  s=("kmm" if dk == NCH - 1 else None))
            E("scalar", "activation", tmpk, bank(b), AF.Relu,
              w=(("kmm", V("kmm")), ("kev", V("kev"))), s="krl")
            E("vector", "tensor_tensor", kact[:, o, lo:lo + TS], tmpk, bank(b),
              ALU.mult, w=(("krl", V("krl")),), s="kev")
        if o + 2 < NCH:
            ffw_load(WkfT_d, o + 2, w=(("kmm", V("kmm")),))
        elif o == NCH - 2:
            ffw_load(WvfT_d, 0, w=(("kmm", V("kmm")),))
        elif o == NCH - 1:
            ffw_load(WvfT_d, 1, w=(("kmm", V("kmm")),))
    kev_all_t = V("kev")
    for o in range(NCH):
        fw_w = (f"fwl{o % 2}", 16 * (2 * 4 + o // 2 + 1))
        for nt in range(NT):
            b = nt % 2
            lo = nt * TS
            for dk in range(NCH):
                E("tensor", "matmul", bank(b), wfb[o % 2][:, dk, :],
                  kact[:, dk, lo:lo + TS], start=(dk == 0), stop=(dk == NCH - 1),
                  w=((fw_w, ("kev", kev_all_t),
                      ("vad", max(0, V("vad") - 1))) if dk == 0 else ()),
                  s=("fvm" if dk == NCH - 1 else None))
            E("vector", "tensor_tensor", tmpv, bank(b), rF[:, o, lo:lo + TS],
              ALU.mult, w=(("fvm", V("fvm")),))
            E("vector", "tensor_add", xB[:, o, lo:lo + TS], xB[:, o, lo:lo + TS],
              tmpv, s="vad")
        if o + 2 < NCH:
            ffw_load(WvfT_d, o + 2, w=(("fvm", V("fvm")),))
    barrier()
    mark("LN4")
    emit_ln(xB, ost, None, "l4")
    for c in range(NCH):
        E("sync", "dma_start", out=out_d[c * 128:(c + 1) * 128, :], in_=ost[:, c, :],
          w=(("lnd", V("lnd")),), s="st", n=16)
    E("sync", "wait_", w=(("st", V("st")),))
    return realize(nc, ops, semnames)


def finish(nc, E, V, ops, semnames, src, out_d, nch):
    for c in range(nch):
        E("sync", "dma_start", out=out_d[c * 128:(c + 1) * 128, :], in_=src[:, c, :],
          s="st", n=16)
    E("sync", "wait_", w=(("st", V("st")),))
    return realize(nc, ops, semnames)


def realize(nc, ops, semnames):
    sems = {nm: nc.alloc_semaphore(name=nm) for nm in sorted(semnames)}

    names = {e: [] for e in ops}
    realize.NAMES = names

    def run(eng_obj, lst):
        eng_key = [k for k, v in ops.items() if v is lst][0]
        for method, args, kw, w, s, n in lst:
            for sn, v in w:
                if v > 0:
                    eng_obj.wait_ge(sems[sn], v)
            if method == "wait_":
                names[eng_key].append(None)
                continue
            ins = getattr(eng_obj, method)(*args, **kw)
            names[eng_key].append(ins.ins.name)
            if s is not None:
                ins.then_inc(sems[s], n)

    with nc.Block() as block:
        @block.tensor
        def _(t):
            run(t, ops["tensor"])

        @block.scalar
        def _(sc):
            run(sc, ops["scalar"])

        @block.vector
        def _(v):
            run(v, ops["vector"])

        @block.sync
        def _(sy):
            run(sy, ops["sync"])

        @block.gpsimd
        def _(g):
            run(g, ops["gpsimd"])
    return nc


def _prep_shared(i):
    f32 = np.float32
    import ml_dtypes
    bf = ml_dtypes.bfloat16
    C = np.ascontiguousarray

    def A(k):
        return np.asarray(i[k], f32)

    d = {}
    d["WiT"] = C(A("L0_attn_Wi").T)
    d["WsT"] = C(A("L0_attn_Ws").T)
    d["WoT"] = C(A("L0_attn_Wo").T)
    d["AT"] = C(A("L0_attn_A").T)
    d["BT"] = C(A("L0_attn_B").T)
    d["CT"] = C(A("L0_attn_C").T)
    d["bi"] = C(A("L0_attn_bi").reshape(128, 1))
    d["bs"] = C(A("L0_attn_bs").reshape(128, 1))
    d["bo"] = C(A("L0_attn_bo").reshape(8, 128).T)
    d["gwT"] = C(A("L0_ffn_gate_w").T)
    d["gateb"] = C(A("L0_ffn_gate_b").reshape(1, 8))
    d["w1"] = C(A("L0_ffn_w1"))
    d["w2"] = C(A("L0_ffn_w2").reshape(8, 2, 2, 8, 128, 8, 128)
                .transpose(0, 1, 5, 2, 4, 3, 6))
    d["b1c"] = C(A("L0_ffn_b1").reshape(8, 32, 128).transpose(2, 0, 1))
    d["b2c"] = C(A("L0_ffn_b2").reshape(8, 8, 128).transpose(2, 0, 1))
    d["WqT"] = C(A("L1_attn_Wq").T)
    d["WkT"] = C(A("L1_attn_Wk").T)
    d["WvT"] = C(A("L1_attn_Wv").T)
    d["WuT"] = C(A("L1_attn_Wu").T)
    d["qlw"] = C(np.tile(A("L1_attn_qln_w").reshape(64, 1), (2, 1)))
    d["qlb"] = C(np.tile(A("L1_attn_qln_b").reshape(64, 1), (2, 1)))
    d["klw"] = C(np.tile(A("L1_attn_kln_w").reshape(64, 1), (2, 1)))
    d["klb"] = C(np.tile(A("L1_attn_kln_b").reshape(64, 1), (2, 1)))
    import ml_dtypes as _mld
    sel = np.zeros((2, 128), _mld.bfloat16)
    sel[0, 0:64] = 1; sel[1, 64:128] = 1
    d["selm"] = sel
    d["mskb"] = C(sel.T)
    d["WrT"] = C(A("L1_ffn_W_r").T)
    d["WkfT"] = C(A("L1_ffn_W_k").T)
    d["WvfT"] = C(A("L1_ffn_W_v").T)
    d["mur"] = C(A("L1_ffn_mu_r").reshape(8, 128).T)
    d["muk"] = C(A("L1_ffn_mu_k").reshape(8, 128).T)
    q = np.arange(128)
    d["maskA"] = np.where(q[None, :] > q[:, None], np.float32(-1e30),
                          np.float32(0.0)).astype(f32)
    d["maskB"] = (q[:, None] <= q[None, :]).astype(bf)
    d["ident"] = np.eye(128, dtype=f32)
    return d


def kernel(**inputs):
    x = np.asarray(inputs["L0_attn_x"], np.float32)
    shared = _prep_shared(inputs)
    nc = build()
    in_maps = []
    for b in range(8):
        m = dict(shared)
        m["xT"] = np.ascontiguousarray(x[b].T)
        in_maps.append(m)
    res = run_bass_kernel_spmd(nc, in_maps, core_ids=list(range(8)))
    out = np.stack([np.ascontiguousarray(np.asarray(r["out"]).T) for r in res.results])
    return out.astype(np.float32)
